# version 16
# baseline (speedup 1.0000x reference)
import math
import numpy as np
import ml_dtypes
import concourse.bass as bass
import concourse.mybir as mybir
from concourse.bass_utils import run_bass_kernel_spmd

F32 = mybir.dt.float32
BF16 = mybir.dt.bfloat16
AF = mybir.ActivationFunctionType
ALU = mybir.AluOpType
AX = mybir.AxisListType
NEG = -3.4e38
BF = ml_dtypes.bfloat16
ENGS = ("tensor", "vector", "scalar", "gpsimd", "sync")


def _host_constants():
    c = {}
    invf = 1.0 / 10000.0 ** (np.arange(0, 64, 2, dtype=np.float64) / 64)
    pos = np.arange(2048, dtype=np.float64)[:, None] * invf[None]
    pos = np.concatenate([pos, pos], axis=1)
    cosT = np.cos(pos).T.astype(np.float32)
    sinT = np.sin(pos).T.astype(np.float32)
    c["cos2"] = np.ascontiguousarray(np.vstack([cosT, cosT]))
    sa = np.vstack([-sinT[0:32], sinT[32:64]])
    c["sin2"] = np.ascontiguousarray(np.vstack([sa, sa]))
    i_idx = np.arange(128)[:, None]
    j_idx = np.arange(256)[None, :]
    basev = (j_idx - i_idx >= 0) & (j_idx - i_idx <= 128)
    m0 = np.where(basev, 0.0, NEG).astype(np.float32)
    m1 = np.where(basev & (j_idx >= 64), 0.0, NEG).astype(np.float32)
    m2 = np.where(basev & (j_idx < 192), 0.0, NEG).astype(np.float32)
    c["lmask"] = np.ascontiguousarray(np.concatenate([m0, m1, m2], axis=1))
    c["idbf"] = np.eye(128, dtype=BF)
    c["idf"] = np.eye(128, dtype=np.float32)
    c["ones_f"] = np.ones((128, 128), np.float32)
    c["ones_b"] = np.ones((1, 128), BF)
    c["iota8"] = np.broadcast_to(np.arange(8, dtype=np.float32), (128, 8)).copy()
    c["im99"] = (c["iota8"] - 99.0).copy()
    sel = np.zeros((8, 8 * 128), BF)
    for e in range(8):
        sel[e, 128 * e:128 * e + 128] = 1.0
    c["sel8"] = sel
    c["fconst"] = np.array([[1e-5, 0.0, 1.0, 0.0, 0.0, 0.0, 0.0, 0.0]], np.float32)
    return c


def _host_layout(inputs):
    g = lambda k: np.ascontiguousarray(np.asarray(inputs[k], dtype=np.float32))
    L = {}
    x = np.asarray(inputs["L0_attn_x"], dtype=np.float32)
    L["_xT"] = [np.ascontiguousarray(x[b].T) for b in range(8)]
    L["wqkv"] = g("L0_attn_light_qkv_w")
    L["wlo"] = g("L0_attn_light_out_w")
    L["whq"] = g("L0_attn_heavy_q_w")
    hkv = g("L0_attn_heavy_kv_w")
    L["wk_t"] = np.ascontiguousarray(hkv[:, 0:64, :].transpose(2, 0, 1).reshape(1024, 512))
    L["wv_t"] = np.ascontiguousarray(hkv[:, 64:128, :].transpose(2, 0, 1).reshape(1024, 512))
    L["whout"] = g("L0_attn_heavy_out_w")
    nkv = g("L0_attn_heavy_null_kv")
    L["nullk"] = np.ascontiguousarray(nkv[0].reshape(4, 128))
    nvrow = np.zeros((1, 520), np.float32)
    for h in range(8):
        nvrow[0, 65 * h:65 * h + 64] = nkv[1, h]
        nvrow[0, 65 * h + 64] = 1.0
    L["nullv"] = nvrow
    L["lng"] = np.ascontiguousarray(g("L0_attn_ln_gamma").reshape(8, 128).T)
    L["lnb"] = np.ascontiguousarray(g("L0_attn_ln_beta").reshape(8, 128).T)
    L["hgam"] = np.ascontiguousarray(g("L0_attn_heavy_gamma").reshape(8, 128).T)
    L["nullq_row"] = np.ascontiguousarray(g("L0_attn_null_q_token").reshape(1, 1024))
    L["toks"] = np.ascontiguousarray(
        np.stack([g("L0_attn_q_route_tok"), g("L0_attn_kv_route_tok")], 1)
        .reshape(8, 128, 2).transpose(1, 0, 2).reshape(128, 16))
    L["wg_t"] = np.ascontiguousarray(g("L0_ffn_Wg").T)
    w1, w3, w2 = g("L0_ffn_W1"), g("L0_ffn_W3"), g("L0_ffn_W2")
    L["w1q"] = np.ascontiguousarray(
        w1.transpose(0, 2, 1).reshape(8, 1024, 4, 512).transpose(0, 2, 1, 3))
    L["w3q"] = np.ascontiguousarray(
        w3.transpose(0, 2, 1).reshape(8, 1024, 4, 512).transpose(0, 2, 1, 3))
    L["w2q"] = np.ascontiguousarray(w2.transpose(0, 2, 1).reshape(8, 4, 512, 1024))
    L["a1t"] = np.ascontiguousarray(g("L0_ffn_A1").transpose(0, 2, 1))
    L["a3t"] = np.ascontiguousarray(g("L0_ffn_A3").transpose(0, 2, 1))
    L["b1tq"] = np.ascontiguousarray(
        g("L0_ffn_B1").transpose(0, 2, 1).reshape(8, 16, 4, 512).transpose(0, 2, 1, 3))
    L["b3tq"] = np.ascontiguousarray(
        g("L0_ffn_B3").transpose(0, 2, 1).reshape(8, 16, 4, 512).transpose(0, 2, 1, 3))
    L["a2tq"] = np.ascontiguousarray(
        g("L0_ffn_A2").transpose(0, 2, 1).reshape(8, 4, 512, 16))
    L["b2t"] = np.ascontiguousarray(g("L0_ffn_B2").transpose(0, 2, 1))
    L["b1c"] = np.ascontiguousarray(g("L0_ffn_b1").reshape(8, 16, 128).transpose(0, 2, 1))
    L["b3c"] = np.ascontiguousarray(g("L0_ffn_b3").reshape(8, 16, 128).transpose(0, 2, 1))
    L["b2all"] = g("L0_ffn_b2")
    return L


def _ap_base(ap):
    t = ap.tensor
    base = getattr(t, "base_partition", 0) or 0
    try:
        off = ap.offset
        w = t.shape[1]
        return int(base + off // w) if w else int(base)
    except Exception:
        return int(base)


class Sem:
    __slots__ = ("h", "n")

    def __init__(self, nc, name):
        self.h = nc.alloc_semaphore(name)
        self.n = 0


class Prog:

    def __init__(self, nc, S):
        self.nc = nc
        self.q = {e: [] for e in ENGS}
        self.ord = {e: S["o_" + e] for e in ("vector", "scalar", "gpsimd")}

    def do(self, eng, emit, waits=(), inc=None, k=1, drain=False):
        w = [(s, n) for (s, n) in waits if n > 0]
        pend = None
        if drain:
            if inc is None:
                inc = self.ord[eng]
                k = 1
            pend = (inc, inc.n + k)
        self.q[eng].append((w, emit, None if inc is None else inc.h, k, pend))
        if getattr(self, "_stage_on", False):
            self.gseq.append(eng)
        if inc is not None:
            inc.n += k

    def dma(self, eng, dst, src, waits=(), inc=None, **kw):
        self.do(eng, lambda E: E.dma_start(dst, src, **kw), waits, inc, 16)

    def begin_stage(self, ordmap):
        self._saveq, self._saveord = self.q, self.ord
        self.q = {e: [] for e in ENGS}
        self.ord = ordmap
        self.gseq = []
        self._stage_on = True

    def end_stage(self):
        st = self.q
        self.q, self.ord = self._saveq, self._saveord
        self._stage_on = False
        return [self.gseq, st, 0]

    def feed(self, stage, frac):
        gseq, st, done = stage
        want = int(math.ceil(len(gseq) * min(1.0, frac)))
        while done < want:
            e = gseq[done]
            self.q[e].append(st[e].pop(0))
            done += 1
        stage[2] = done

    def flush(self, name):
        qs = {e: ops for e, ops in self.q.items() if ops}
        if not qs:
            return
        with self.nc.Block(name) as blk:
            for ename, ops in qs.items():
                def mk(ops):
                    def body(E):
                        seen = {}
                        prev_pend = [None]
                        for (waits, emit, inch, k, pend) in ops:
                            allw = list(waits)
                            if prev_pend[0] is not None:
                                allw.append(prev_pend[0])
                            for (s, n) in allw:
                                h = s.h
                                if seen.get(id(h), -1) < n:
                                    E.wait_ge(h, n)
                                    seen[id(h)] = n
                            ins = emit(E)
                            if inch is not None:
                                ins.then_inc(inch, k)
                            prev_pend[0] = (pend[0], pend[1]) if pend else None
                    return body
                getattr(blk, ename)(mk(ops))
        self.q = {e: [] for e in ENGS}


def build(debug_taps=()):
    nc = bass.Bass("TRN2", target_bir_lowering=False)

    dr = {}
    def din(name, shape, dt=F32):
        dr[name] = nc.dram_tensor(name, list(shape), dt, kind="ExternalInput").ap()
        return dr[name]

    xT = din("xT", (1024, 2048))
    out_d = nc.dram_tensor("out", [1024, 2048], F32, kind="ExternalOutput").ap()
    for nm, shp, dt in [
            ("wqkv", (1024, 1536), F32), ("wlo", (512, 1024), F32),
            ("whq", (1024, 512), F32), ("wk_t", (1024, 512), F32),
            ("wv_t", (1024, 512), F32), ("whout", (512, 1024), F32),
            ("nullk", (4, 128), F32), ("nullv", (1, 520), F32),
            ("lng", (128, 8), F32), ("lnb", (128, 8), F32), ("hgam", (128, 8), F32),
            ("nullq_row", (1, 1024), F32), ("toks", (128, 16), F32),
            ("wg_t", (1024, 8), F32),
            ("w1q", (8, 4, 1024, 512), F32), ("w3q", (8, 4, 1024, 512), F32),
            ("w2q", (8, 4, 512, 1024), F32),
            ("a1t", (8, 1024, 16), F32), ("a3t", (8, 1024, 16), F32),
            ("b1tq", (8, 4, 16, 512), F32), ("b3tq", (8, 4, 16, 512), F32),
            ("a2tq", (8, 4, 512, 16), F32), ("b2t", (8, 16, 1024), F32),
            ("b1c", (8, 128, 16), F32), ("b3c", (8, 128, 16), F32),
            ("b2all", (8, 1024), F32),
            ("cos2", (128, 2048), F32), ("sin2", (128, 2048), F32),
            ("lmask", (128, 768), F32), ("iota8", (128, 8), F32),
            ("im99", (128, 8), F32),
            ("idbf", (128, 128), BF16), ("idf", (128, 128), F32),
            ("ones_f", (128, 128), F32), ("ones_b", (1, 128), BF16),
            ("sel8", (8, 1024), BF16), ("fconst", (1, 8), F32)]:
        din(nm, shp, dt)

    BASE = (nc.sbuf_base + 31) & ~31
    LIMIT = BASE + 212_800
    cur = [BASE]

    def sq(name, shape, dt):
        a = cur[0]
        sz = shape[1] * (4 if dt == F32 else 2)
        cur[0] = a + ((sz + 31) & ~31)
        assert cur[0] <= LIMIT, f"SBUF overflow at {name}"
        return nc.alloc_sbuf_tensor_at(name, list(shape), dt, offset=a), a

    def sat(name, shape, dt, a):
        esz = 4 if dt == F32 else 2
        assert a + shape[1] * esz <= LIMIT, f"{name} beyond limit"
        return nc.alloc_sbuf_tensor_at(name, list(shape), dt, offset=a)

    rows1, ROWS = sq("rows1", [128, 4096], F32)
    cos2, COSA = sq("cos2", [128, 2048], F32)
    sin2, SINA = sq("sin2", [128, 2048], F32)
    lmask, LMA = sq("lmask", [128, 768], F32)
    idbf, _ = sq("idbf", [128, 128], BF16)
    idf, _ = sq("idf", [128, 128], F32)
    ones_f, OFA = sq("ones_f", [128, 128], F32)
    ones_b, _ = sq("ones_b", [1, 128], BF16)
    ones_bc, _ = sq("ones_bc", [128, 16], BF16)
    gam4, _ = sq("gam4", [128, 24], F32)
    toks, _ = sq("toks", [128, 16], F32)
    iota8, _ = sq("iota8", [128, 8], F32)
    im99, _ = sq("im99", [128, 8], F32)
    sel8, _ = sq("sel8", [8, 1024], BF16)
    nqbf, _ = sq("nqbf", [1, 1024], BF16)
    nmqbf, _ = sq("nmqbf", [1, 2048], BF16)
    smwk, _ = sq("smwk", [128, 8], F32)
    rrow, _ = sq("rrow", [1, 256], F32)
    rrbf, _ = sq("rrbf", [1, 256], BF16)
    b2allb, _ = sq("b2allb", [8, 1024], BF16)
    gatesF, _ = sq("gatesF", [8, 2048], BF16)
    wgsum, _ = sq("wgsum", [8, 8], F32)
    gtok, GTOK = sq("gtok", [128, 64], F32)
    kcol, _ = sq("kcol", [128, 32], F32)
    rankc, _ = sq("rankc", [128, 32], F32)
    mtok, _ = sq("mtok", [128, 32], F32)
    fconst, _ = sq("fconst", [1, 8], F32)
    bc0, BC0 = sq("bc0", [128, 2048], F32)
    bc1, BC1 = sq("bc1", [128, 2048], F32)
    lightT, LT = sq("lightT", [128, 8192], BF16)
    heavyT, HT = sq("heavyT", [128, 8192], BF16)
    ws, WS = sq("ws", [128, 2048], F32)
    big, BIG = sq("big", [128, 24576], F32)
    assert cur[0] <= LIMIT

    xc = [sat(f"xc{i}", [128, 2048], F32, BIG + 8192 * i) for i in range(2)]
    xsq = sat("xsq", [128, 2048], F32, BIG + 16384)
    xlc = [sat(f"xlc{i}", [128, 2048], F32, BIG + 24576 + 8192 * i) for i in range(2)]
    xlt = sat("xlt", [128, 2048], F32, BIG + 32768)
    mB = sat("mB", [128, 2048], F32, BIG + 40960)
    rB = sat("rB", [128, 2048], F32, BIG + 49152)
    c0B = sat("c0B", [128, 2048], F32, BIG + 57344)
    c1B = sat("c1B", [128, 2048], F32, BIG + 84096)
    ln2T = sat("ln2T", [128, 16384], BF16, BIG + 0)
    lq = sat("lq", [128, 8192], BF16, BIG + 32768)
    lk = sat("lk", [128, 8704], BF16, BIG + 49152)
    vsh = sat("vsh", [128, 8704], BF16, BIG + 66560)
    lsim = [sat(f"lsim{i}", [128, 256], F32, BIG + 84096 + 1024 * i) for i in range(2)]
    lattnT = [sat(f"lattnT{i}", [128, 256], BF16, BIG + 86144 + 512 * i)
              for i in range(2)]
    lattnTT = [sat(f"lattnTT{i}", [128, 256], BF16, BIG + 87168 + 512 * i)
               for i in range(2)]
    cmpw = sat("cmpw", [128, 2048], F32, BIG + 88192)
    kbkv = sat("kbkv", [128, 2048], F32, BIG + 24576)
    cscr_bf = sat("cscr_bf", [128, 2048], BF16, BIG + 0)
    xrT = sat("xrT", [128, 16384], BF16, BIG + 0)
    mB2 = sat("mB2", [128, 2048], F32, BIG + 32768)
    rsB = sat("rsB", [128, 2048], F32, BIG + 84096)
    hq = sat("hq", [128, 8192], BF16, BIG + 32768)
    hk = sat("hk", [128, 8256], BF16, BIG + 49152)
    hv = sat("hv", [128, 8840], BF16, BIG + 66560)
    attnbf = sat("attnbf", [128, 2052], BF16, BIG + 84288)
    attnT = sat("attnT", [128, 2176], BF16, BIG + 88416)
    ropea = sat("ropea", [128, 512], F32, BIG + 92800)
    ropeb = sat("ropeb", [128, 512], F32, BIG + 94848)
    rbsb = sat("rbsb", [64, 128], F32, BIG + 96896)
    denrow = sat("denrow", [1, 2048], F32, WS)
    rbig = sat("rbig", [128, 2048], F32, COSA)

    x1 = sat("x1", [128, 16384], F32, BIG + 0)
    xs2 = [sat(f"xs2_{i}", [128, 2048], F32, BIG + 65536 + 8192 * i) for i in range(2)]
    xl2T = sat("xl2T", [128, 16384], BF16, BIG + 65536)
    whofull = sat("whofull", [128, 4096], BF16, BIG + 84288)
    fsq = sat("fsq", [128, 2048], F32, COSA)
    m3B = sat("m3B", [128, 2048], F32, BC0)
    r3B = sat("r3B", [128, 2048], F32, BC1)
    mrst = [sat(f"mrst{i}", [128, 512], F32, HT + 2048 * i) for i in range(2)]
    gtmp1 = sat("gtmp1", [128, 512], BF16, SINA)
    gtmp2 = sat("gtmp2", [128, 512], BF16, SINA + 1024)
    gateb = sat("gateb", [128, 2048], BF16, SINA + 2048)
    gTq = sat("gTq", [128, 2048], BF16, LMA)

    def rowt(name, pbase, cole, parts, cols, dt=F32, arena=ROWS):
        esz = 4 if dt == F32 else 2
        t = nc.alloc_sbuf_tensor_at(name, [pbase + parts, cols], dt,
                                    offset=arena + cole * esz)
        return t[pbase:pbase + parts, :]

    m_row = rowt("m_row", 0, 0, 1, 2048)
    r_row = rowt("r_row", 0, 2048, 1, 2048)
    c0_row = rowt("c0_row", 32, 0, 1, 2048)
    c1_row = rowt("c1_row", 32, 2048, 1, 2048)
    sr_row = rowt("sr_row", 0, 0, 1, 2048, F32, HT)
    t_row = rowt("t_row", 64, 2048, 1, 2048)
    selqbf32 = sat("selqb32", [1, 2048], F32, BIG + 92800)
    s_rt = rowt("s_rt", 0, 0, 2, 2048, F32, BC0)
    rl_rt = rowt("rl_rt", 0, 0, 2, 2048, F32, BC1)
    u_rt = rowt("u_rt", 0, 0, 2, 2048, F32, HT + 8192)
    key_rt = rowt("key_rt", 0, 0, 2, 2048, F32, BIG + 8192)
    z_rt = rowt("z_rt", 0, 0, 2, 2048, F32, BIG + 88192)
    zp_rt = rowt("zp_rt", 0, 0, 2, 2048, F32, WS)
    rank_rt = rowt("rank2", 0, 0, 2, 2048, F32, BC1)
    t32 = rowt("t32", 32, 0, 1, 2048, F32, BC1)
    sel_rt = rowt("sel3", 0, 0, 2, 2048, F32, BC0)
    sc_rt = rowt("sc_rt", 0, 0, 2, 16, F32, GTOK)
    lgsb = rowt("lgsb", 0, 0, 8, 2048, F32, SINA)
    mr8 = rowt("mr8", 0, 0, 8, 2048, F32, LT)
    rr8 = rowt("rr8", 0, 0, 8, 2048, F32, LT + 8192)
    u1sb = rowt("u1sb", 0, 0, 16, 2048, BF16, ROWS)
    u3sb = rowt("u3sb", 0, 2048, 16, 2048, BF16, ROWS)
    u2sb = rowt("u2sb", 32, 0, 16, 2048, F32, ROWS)
    u2bf = rowt("u2bf", 0, 4096, 16, 2048, BF16, ROWS)
    b2te = sat("b2te", [16, 1024], BF16, WS)
    b1tqb = [sat(f"b1tqb{i}", [16, 512], BF16, WS + 2048 + 1024 * i) for i in range(2)]
    b3tqb = [sat(f"b3tqb{i}", [16, 512], BF16, WS + 4096 + 1024 * i) for i in range(2)]
    a2tqb = [sat(f"a2tqb{i}", [128, 64], BF16, WS + 6144 + 128 * i) for i in range(2)]
    a1te = sat("a1te", [128, 128], BF16, WS + 6400)
    a3te = sat("a3te", [128, 128], BF16, WS + 6656)
    b1ce = sat("b1ce", [128, 16], F32, WS + 6912)
    b3ce = sat("b3ce", [128, 16], F32, WS + 6976)
    wst = [sat(f"wst{i}", [128, 1024], BF16, WS + 2048 + 2048 * i) for i in range(2)]
    wvfull = sat("wvfull", [128, 4096], BF16, WS)
    wlofull = sat("wlofull", [128, 4096], BF16, WS)
    w1b = [sat("w1b0", [128, 4096], BF16, BC0), sat("w1b1", [128, 4096], BF16, LT)]
    w3b = [sat("w3b0", [128, 4096], BF16, BC1),
           sat("w3b1", [128, 4096], BF16, LT + 8192)]
    w2b = [sat("w2b0", [128, 4096], BF16, HT), sat("w2b1", [128, 4096], BF16, HT + 8192)]

    pSIM = nc.alloc_psum_tensor("pSIM", [128, 2048], F32)
    pT = nc.alloc_psum_tensor("pT", [128, 2048], BF16)
    pOUT = nc.alloc_psum_tensor("pOUT", [128, 128], F32)
    pMISC = nc.alloc_psum_tensor("pMISC", [128, 512], F32)
    def pch(i):
        return pSIM[:, 512 * i:512 * i + 512]

    names = ["sd0", "sd1", "sd2", "sdc", "sgc", "sgs0", "sgs1", "sgw", "sgn", "sgf",
             "sgp0", "sgp1", "mm", "mm2", "mm3", "mm4", "vec", "vec2", "vec3",
             "scl", "scl2", "pool", "sim", "mx", "ex", "tp", "tes", "tev",
             "av", "rr", "rb", "oe", "fin",
             "c1p", "c1v", "c1w", "c1s", "c1m", "c1x", "c1d"]
    names += ["o_vector", "o_scalar", "o_gpsimd",
              "c1ov", "c1os", "c1og"]
    S = {k: Sem(nc, k) for k in names}
    nc.const_aps.register(F32, 1e-5, fconst[0:1, 0:1])
    P = Prog(nc, S)
    TAPS = {}
    PEND = set()

    def tap(nm, t):
        ap = t if isinstance(t, bass.AP) else t[:, :]
        shp = [int(x) for x in ap.shape]
        tdr = nc.dram_tensor(f"tap_{nm}", shp, ap.dtype, kind="ExternalOutput").ap()
        P.dma("sync", tdr, ap,
              waits=[(s, s.n) for s in S.values()
                     if s.n > 0 and s not in PEND], inc=S["fin"])
        for _e in ("tensor", "vector", "scalar", "gpsimd"):
            P.do(_e, lambda E: E.nop(), waits=[(S["fin"], S["fin"].n)])
        TAPS[nm] = True
    xch = lambda i: xc[i % 2]
    xlch = lambda i: xlc[i % 2]

    for t, d in [(cos2[:, :], dr["cos2"]), (sin2[:, :], dr["sin2"]),
                 (idf[:, :], dr["idf"]), (ones_f[:, :], dr["ones_f"]),
                 (iota8[:, :], dr["iota8"]), (im99[:, :], dr["im99"]),
                 (gam4[:, 0:8], dr["lng"]), (gam4[:, 8:16], dr["lnb"]),
                 (gam4[:, 16:24], dr["hgam"]),
                 (toks[:, :], dr["toks"]),
                 (lmask[:, :], dr["lmask"]),
                 (idbf[:, :], dr["idbf"]), (ones_b[:, :], dr["ones_b"]),
                 (sel8[:, :], dr["sel8"]), (fconst[:, :], dr["fconst"])]:
        P.dma("sync", t, d, inc=S["sdc"])
    P.dma("gpsimd", nqbf[:, :], dr["nullq_row"], inc=S["sgc"])
    P.dma("gpsimd", b2allb[:, :], dr["b2all"], inc=S["sgc"])
    P.dma("gpsimd", ones_bc[:, :], dr["ones_f"][:, 0:16], inc=S["sgc"])
    SD_CONST = S["sdc"].n
    for _e in ("tensor", "vector", "scalar", "gpsimd"):
        P.do(_e, lambda E: E.nop(),
             waits=[(S["sdc"], SD_CONST), (S["sgc"], S["sgc"].n)])

    def row_stats(dst_m, dst_r):
        P.do("vector", lambda E: E.tensor_scalar_mul(dst_m, pSIM[0:1, 0:2048],
                                                     1.0 / 1024.0),
             waits=[(S["mm"], S["mm"].n)], inc=S["vec"], drain=True)
        P.do("vector", lambda E: E.tensor_mul(dst_r, dst_m, dst_m),
             inc=S["vec"], drain=True)
        P.do("vector", lambda E: E.scalar_tensor_tensor(
            t_row, pSIM[32:33, 0:2048], 1.0 / 1024.0, dst_r,
            ALU.mult, ALU.subtract), inc=S["vec"], drain=True)
        P.do("scalar", lambda E: E.activation(t_row, t_row, AF.Sqrt, bias=1e-5),
             waits=[(S["vec"], S["vec"].n), (S["sdc"], SD_CONST)],
             inc=S["scl"], drain=True)
        P.do("vector", lambda E: E.reciprocal(dst_r, t_row),
             waits=[(S["scl"], S["scl"].n)], inc=S["vec"], drain=True)

    def bcast(src_row, dst, wait=()):
        b = src_row.tensor.base_partition + src_row.offset // \
            (src_row.tensor.shape[1]) if False else 0
        try:
            b = int(src_row.ap[0][0] // max(1, src_row.tensor.shape[1])) if False else 0
        except Exception:
            b = 0
        b = _ap_base(src_row)
        def mmf(E):
            ins = None
            for tc in range(4):
                ins = E.matmul(pch(tc), ones_f[b:b + 1, :],
                               src_row[:, 512 * tc:512 * tc + 512],
                               start=True, stop=True)
            return ins
        P.do("tensor", mmf,
             waits=[(S["vec"], S["vec"].n), (S["scl"], S["scl"].n)] + list(wait),
             inc=S["mm"])
        P.do("scalar", lambda E: E.copy(dst[:, :], pSIM[:, 0:2048]),
             waits=[(S["mm"], S["mm"].n)], inc=S["scl"], drain=True)

    def xload(dst, dc, waits, alt=False):
        if alt and dc % 2:
            sem, eng = S["sd2"], "gpsimd"
        else:
            sem, eng = S[f"sd{dc % 2}"], "sync"
        P.dma(eng, dst, xT[128 * dc:128 * dc + 128, :], waits=waits, inc=sem)
        return [(sem, sem.n)]
    for dc in range(8):
        ld = xload(xch(dc)[:, :], dc, [(S["mm"], max(0, S["mm"].n - 1))],
                   alt=True)
        P.do("scalar", (lambda d=dc: lambda E:
                        E.activation(xsq[:, :], xch(d)[:, :], AF.Square))(),
             waits=ld + [(S["mm"], S["mm"].n)], inc=S["scl"], drain=True)
        def mk(d=dc):
            def f(E):
                ins = None
                for tc in range(4):
                    ins = E.matmul(pSIM[0:1, 512 * tc:512 * tc + 512], ones_f[:, 0:1],
                                   xch(d)[:, 512 * tc:512 * tc + 512],
                                   start=(d == 0), stop=(d == 7), skip_group_check=True)
                for tc in range(4):
                    ins = E.matmul(pSIM[32:33, 512 * tc:512 * tc + 512], ones_f[:, 1:2],
                                   xsq[:, 512 * tc:512 * tc + 512],
                                   start=(d == 0), stop=(d == 7), skip_group_check=True)
                return ins
            return f
        P.do("tensor", mk(), waits=ld + [(S["scl"], S["scl"].n)],
             inc=S["mm"])
    row_stats(m_row, r_row)
    bcast(m_row, mB)
    bcast(r_row, rB)
    P.flush("p1")

    for dc in range(8):
        ld = xload(xch(dc)[:, :], dc, [(S["vec"], max(0, S["vec"].n - 1))],
                   alt=True)
        P.do("vector", (lambda d=dc: lambda E:
                        E.tensor_sub(xlch(d)[:, :], xch(d)[:, :], mB[:, :]))(),
             waits=ld + [(S["scl"], S["scl"].n),
                         (S["mm"], max(0, S["mm"].n - 1))],
             inc=S["vec"], drain=True)
        P.do("vector", (lambda d=dc: lambda E:
                        E.tensor_mul(xlch(d)[:, :], xlch(d)[:, :], rB[:, :]))(),
             inc=S["vec"], drain=True)
        P.do("scalar", (lambda d=dc: lambda E:
                        E.activation(xsq[:, :], xlch(d)[:, :], AF.Square))(),
             waits=[(S["vec"], S["vec"].n), (S["mm"], S["mm"].n)],
             inc=S["scl"], drain=True)
        def mk(d=dc):
            def f(E):
                ins = None
                for tc in range(4):
                    ins = E.matmul(pSIM[0:1, 512 * tc:512 * tc + 512], ones_f[:, 0:1],
                                   xlch(d)[:, 512 * tc:512 * tc + 512],
                                   start=(d == 0), stop=(d == 7), skip_group_check=True)
                for tc in range(4):
                    ins = E.matmul(pSIM[64:66, 512 * tc:512 * tc + 512],
                                   toks[:, 2 * d:2 * d + 2],
                                   xlch(d)[:, 512 * tc:512 * tc + 512],
                                   start=(d == 0), stop=(d == 7), skip_group_check=True)
                for tc in range(4):
                    ins = E.matmul(pSIM[32:33, 512 * tc:512 * tc + 512], ones_f[:, 1:2],
                                   xsq[:, 512 * tc:512 * tc + 512],
                                   start=(d == 0), stop=(d == 7), skip_group_check=True)
                return ins
            return f
        P.do("tensor", mk(), waits=[(S["scl"], S["scl"].n)], inc=S["mm"])
    P.do("vector", lambda E: E.tensor_copy(s_rt, pSIM[64:66, 0:2048]),
         waits=[(S["mm"], S["mm"].n)], inc=S["vec"], drain=True)
    P.do("vector", lambda E: E.tensor_scalar_mul(c0_row, pSIM[0:1, 0:2048], 1 / 1024.0),
         inc=S["vec"], drain=True)
    P.do("vector", lambda E: E.tensor_copy(sr_row, pSIM[32:33, 0:2048]),
         inc=S["vec"], drain=True)
    P.do("vector", lambda E: E.tensor_mul(c1_row, c0_row, c0_row),
         inc=S["vec"], drain=True)
    P.do("vector", lambda E: E.scalar_tensor_tensor(
        t_row, pSIM[32:33, 0:2048], 1.0 / 1024.0, c1_row,
        ALU.mult, ALU.subtract), inc=S["vec"], drain=True)
    P.do("scalar", lambda E: E.activation(t_row, t_row, AF.Sqrt, bias=1e-5),
         waits=[(S["vec"], S["vec"].n)], inc=S["scl"], drain=True)
    P.do("scalar", lambda E: E.activation(sr_row, sr_row, AF.Sqrt),
         inc=S["scl"], drain=True)
    P.do("vector", lambda E: E.reciprocal(c1_row, t_row),
         waits=[(S["scl"], S["scl"].n)], inc=S["vec"], drain=True)
    P.do("vector", lambda E: E.tensor_scalar_max(sr_row, sr_row, 1e-12),
         inc=S["vec"], drain=True)
    P.do("vector", lambda E: E.reciprocal(sr_row, sr_row), inc=S["vec"], drain=True)
    P.do("vector", lambda E: E.tensor_scalar_mul(sr_row, sr_row, 32.0),
         inc=S["vec"], drain=True)
    P.do("vector", lambda E: E.tensor_mul(c0_row, c0_row, c1_row),
         inc=S["vec"], drain=True)
    P.do("vector", lambda E: E.tensor_scalar_mul(c0_row, c0_row, -1.0),
         inc=S["vec"], drain=True)
    bcast(c0_row, c0B)
    bcast(c1_row, c1B)
    P.flush("p2")
    if "route_s" in debug_taps:
        tap("s_rt", s_rt)

    c1ord = {"vector": S["c1ov"], "scalar": S["c1os"], "gpsimd": S["c1og"]}
    P.begin_stage(c1ord)
    P.do("gpsimd", lambda E: E.tensor_copy(rl_rt, s_rt),
         waits=[(S["vec"], S["vec"].n)], inc=S["c1p"], drain=True)
    cur_eps = 4.0
    logk = math.log(1152.0)
    for it in range(20):
        inv = 1.0 / cur_eps
        P.do("gpsimd", lambda E: E.tensor_sub(u_rt, s_rt, rl_rt),
             waits=[(S["c1s"], S["c1s"].n), (S["c1p"], S["c1p"].n)],
             inc=S["c1p"], drain=True)
        P.do("vector", lambda E: E.reduce_max(sc_rt[:, 0:1], u_rt, axis=AX.X),
             waits=[(S["c1p"], S["c1p"].n)], inc=S["c1v"], drain=True)
        P.do("vector", (lambda inv=inv: lambda E:
                        E.tensor_scalar(sc_rt[:, 1:2], sc_rt[:, 0:1], -inv, None,
                                        ALU.mult))(),
             inc=S["c1v"], drain=True)
        P.do("scalar", (lambda inv=inv: lambda E:
                        E.activation(rl_rt, u_rt, AF.Exp, bias=sc_rt[:, 1:2],
                                     scale=inv))(),
             waits=[(S["c1v"], S["c1v"].n)], inc=S["c1s"], drain=True)
        P.do("vector", lambda E: E.reduce_sum(sc_rt[:, 2:3], rl_rt, axis=AX.X),
             waits=[(S["c1s"], S["c1s"].n)], inc=S["c1v"], drain=True)
        P.do("scalar", lambda E: E.activation(sc_rt[:, 3:4], sc_rt[:, 2:3], AF.Ln),
             waits=[(S["c1v"], S["c1v"].n)], inc=S["c1s"], drain=True)
        P.do("gpsimd", lambda E: E.tensor_sub(sc_rt[:, 4:5], sc_rt[:, 3:4],
                                              sc_rt[:, 1:2]),
             waits=[(S["c1s"], S["c1s"].n)], inc=S["c1p"], drain=True)
        P.do("gpsimd", (lambda c=cur_eps: lambda E:
                        E.tensor_scalar(sc_rt[:, 5:6], sc_rt[:, 4:5], -c, c * logk,
                                        ALU.mult, ALU.add))(),
             inc=S["c1p"], drain=True)
        P.do("scalar", lambda E: E.activation(rl_rt, s_rt, AF.Relu,
                                              bias=sc_rt[:, 5:6], scale=1.0),
             waits=[(S["c1p"], S["c1p"].n)], inc=S["c1s"], drain=True)
        cur_eps = max(cur_eps * 0.7, 0.03)
    c1s1 = P.end_stage()
    PEND.update(S[k] for k in ("c1p", "c1v", "c1w", "c1s", "c1m", "c1x", "c1d",
                               "c1ov", "c1os", "c1og"))

    for dc in range(8):
        ld = xload(xs2[dc % 2][:, :], dc, [(S["vec"], max(0, S["vec"].n - 4))],
                   alt=True)
        P.do("vector", (lambda d=dc: lambda E:
                        E.tensor_sub(xlt[:, :], xs2[d % 2][:, :], mB[:, :]))(),
             waits=ld + [(S["scl"], S["scl"].n)], inc=S["vec"], drain=True)
        P.do("vector", lambda E: E.tensor_mul(xlt[:, :], xlt[:, :], rB[:, :]),
             inc=S["vec"], drain=True)
        P.do("vector", lambda E: E.tensor_mul(xlt[:, :], xlt[:, :], c1B[:, :]),
             inc=S["vec"], drain=True)
        P.do("vector", lambda E: E.tensor_add(xlt[:, :], xlt[:, :], c0B[:, :]),
             inc=S["vec"], drain=True)
        P.do("vector", (lambda d=dc: lambda E:
                        E.tensor_scalar(ln2T[:, 2048 * d:2048 * d + 2048], xlt[:, :],
                                        gam4[:, d:d + 1], gam4[:, 8 + d:8 + d + 1],
                                        ALU.mult, ALU.add))(),
             inc=S["vec"], drain=True)
    P.flush("p2b")
    if "ln2T" in debug_taps:
        tap("ln2T", ln2T)

    P.dma("gpsimd", wvfull[:, :],
          dr["wqkv"][:, 1024:1536].rearrange("(c p) f -> p c f", p=128),
          inc=S["sgw"])
    P.do("vector", lambda E: E.memset(lk[:, :], 0.0), inc=S["vec"], drain=True)
    P.do("vector", lambda E: E.memset(vsh[:, :], 0.0), inc=S["vec"], drain=True)
    vbar = S["vec"].n
    sgv = S["sgw"].n
    for tt in range(16):
        def mk(tt=tt):
            def f(E):
                ins = None
                for dc in range(8):
                    ins = E.matmul(pch(tt % 2),
                                   ln2T[:, 2048 * dc + 128 * tt:2048 * dc + 128 * tt + 128],
                                   wvfull[:, 512 * dc:512 * dc + 512],
                                   start=(dc == 0), stop=(dc == 7))
                return ins
            return f
        P.do("tensor", mk(), waits=[(S["sgw"], sgv), (S["vec"], vbar),
                                    (S["vec2"], max(0, S["vec2"].n - 1))],
             inc=S["mm"])
        P.do("vector", (lambda tt=tt: lambda E:
                        E.tensor_copy(vsh[64:128, 512 * tt:512 * tt + 512],
                                      pch(tt % 2)[0:64, :]))(),
             waits=[(S["mm"], S["mm"].n)], inc=S["vec3"], drain=True)
        P.do("vector", (lambda tt=tt: lambda E:
                        E.tensor_copy(vsh[0:64, 512 * (tt + 1):512 * (tt + 1) + 512],
                                      pch(tt % 2)[64:128, :]))(),
             inc=S["vec2"], drain=True)
    for fc in range(8):
        wsem = S[f"sgs{fc % 2}"]
        P.dma("gpsimd", wst[fc % 2][:, :],
              dr["wqkv"][:, 128 * fc:128 * fc + 128]
              .rearrange("(c p) f -> p c f", p=128),
              waits=[(S["mm2"], max(0, S["mm2"].n - 4)), (S["mm"], S["mm"].n)],
              inc=wsem)
        sgn = wsem.n
        for tc in range(4):
            k = 4 * fc + tc
            def mk(fc=fc, tc=tc, k=k):
                def f(E):
                    ins = None
                    for dc in range(8):
                        ins = E.matmul(pch(k % 2),
                                       wst[fc % 2][:, 128 * dc:128 * dc + 128],
                                       ln2T[:, 2048 * dc + 512 * tc:
                                            2048 * dc + 512 * tc + 512],
                                       start=(dc == 0), stop=(dc == 7))
                    return ins
                return f
            P.do("tensor", mk(), waits=[(wsem, sgn),
                                        (S["vec"], max(vbar, S["vec"].n - 1)),
                                        (S["vec2"], S["vec2"].n)],
                 inc=S["mm2"])
            if fc < 4:
                dst = lq[:, 2048 * fc + 512 * tc:2048 * fc + 512 * tc + 512]
            else:
                hp = fc - 4
                dst = lk[:, 2176 * hp + 64 + 512 * tc:2176 * hp + 64 + 512 * tc + 512]
            P.do("vector", (lambda d=dst, k=k: lambda E: E.tensor_copy(d, pch(k % 2)))(),
                 waits=[(S["mm2"], S["mm2"].n)], inc=S["vec"], drain=True)
    P.flush("b2")
    if "lqkv" in debug_taps:
        tap("lq", lq); tap("lk", lk); tap("vsh", vsh)

    P.begin_stage({"vector": S["c1ov"], "scalar": S["c1os"],
                   "gpsimd": S["c1og"]})

    def c1bcast(src_row, dst, wait=()):
        b = _ap_base(src_row)
        for tc in range(4):
            P.do("tensor", (lambda b=b, tc=tc: lambda E:
                            E.matmul(pMISC[:, 0:512], ones_f[b:b + 1, :],
                                     src_row[:, 512 * tc:512 * tc + 512],
                                     start=True, stop=True))(),
                 waits=[(S["c1s"], S["c1s"].n),
                        (S["c1w"], S["c1w"].n)] + list(wait),
                 inc=S["c1m"])
            P.do("scalar", (lambda dst=dst, tc=tc: lambda E:
                            E.copy(dst[:, 512 * tc:512 * tc + 512],
                                   pMISC[:, 0:512]))(),
                 waits=[(S["c1m"], S["c1m"].n)], inc=S["c1s"], drain=True)

    P.do("gpsimd", lambda E: E.tensor_scalar(key_rt, s_rt, sc_rt[:, 5:6], 0.0,
                                             ALU.add, ALU.min),
         waits=[(S["c1s"], S["c1s"].n), (S["mm"], S["mm"].n),
                (S["mm2"], S["mm2"].n)],
         inc=S["c1p"], drain=True)
    P.do("gpsimd", lambda E: E.tensor_scalar(z_rt, key_rt, 0.0, None, ALU.is_equal),
         inc=S["c1p"], drain=True)
    P.do("gpsimd", lambda E: E.memset(u_rt, 1.0), inc=S["c1p"], drain=True)
    P.do("vector", lambda E: E.tensor_tensor_scan(zp_rt, u_rt, z_rt, 0.0,
                                                  ALU.mult, ALU.add),
         waits=[(S["c1p"], S["c1p"].n)], inc=S["c1v"], drain=True)
    P.do("vector", lambda E: E.tensor_sub(zp_rt, zp_rt, z_rt), inc=S["c1v"], drain=True)
    P.do("vector", lambda E: E.tensor_mul(zp_rt, zp_rt, z_rt), inc=S["c1v"], drain=True)
    c1bcast(key_rt[0:1, :], cmpw, wait=[(S["c1v"], S["c1v"].n)])
    for c in range(16):
        P.do("tensor", (lambda c=c: lambda E:
                        E.transpose(pMISC[0:128, 0:2],
                                    key_rt[:, 128 * c:128 * c + 128], idf[0:2, 0:2]))(),
             waits=[(S["c1s"], S["c1s"].n), (S["c1w"], S["c1w"].n)],
             inc=S["c1m"])
        P.do("vector", (lambda c=c: lambda E:
                        E.tensor_copy(kcol[:, 2 * c:2 * c + 2], pMISC[0:128, 0:2]))(),
             waits=[(S["c1m"], S["c1m"].n)], inc=S["c1w"], drain=True)
    for c in range(16):
        P.do("tensor", (lambda c=c: lambda E:
                        E.transpose(pMISC[0:1, 0:128], kcol[:, 2 * c + 1:2 * c + 2],
                                    idf[:, :]))(),
             waits=[(S["c1w"], S["c1w"].n)], inc=S["c1m"])
        P.do("vector", (lambda c=c: lambda E:
                        E.tensor_copy(t_row[:, 128 * c:128 * c + 128],
                                      pMISC[0:1, 0:128]))(),
             waits=[(S["c1m"], S["c1m"].n)], inc=S["c1w"], drain=True)
    c1bcast(t_row, kbkv, wait=[(S["c1w"], S["c1w"].n)])
    KBS, KCW = S["c1s"].n, S["c1w"].n
    for tc in range(4):
        for cp in range(16):
            for r in range(2):
                kb = cmpw if r == 0 else kbkv
                slot = (cp % 2) * 2 + r
                P.do("vector", (lambda kb=kb, cp=cp, r=r, tc=tc, slot=slot:
                                lambda E:
                                E.tensor_scalar(
                                    cscr_bf[:, 512 * slot:512 * slot + 512],
                                    kb[:, 512 * tc:512 * tc + 512],
                                    kcol[:, 2 * cp + r:2 * cp + r + 1], None,
                                    ALU.is_lt))(),
                     waits=[(S["c1w"], KCW), (S["c1s"], KBS),
                            (S["c1m"], max(0, S["c1m"].n - 3))],
                     inc=S["c1v"], drain=True)
                P.do("tensor", (lambda cp=cp, r=r, slot=slot: lambda E:
                                E.matmul(pMISC[32 * r:32 * r + 1, 0:512],
                                         ones_bc[:, 0:1],
                                         cscr_bf[:, 512 * slot:512 * slot + 512],
                                         start=(cp == 0), stop=(cp == 15),
                                         skip_group_check=True))(),
                     waits=[(S["c1v"], S["c1v"].n), (S["c1w"], S["c1w"].n)],
                     inc=S["c1m"])
        P.do("vector", (lambda tc=tc: lambda E:
                        E.tensor_copy(rank_rt[0:1, 512 * tc:512 * tc + 512],
                                      pMISC[0:1, 0:512]))(),
             waits=[(S["c1m"], S["c1m"].n)], inc=S["c1w"], drain=True)
        P.do("vector", (lambda tc=tc: lambda E:
                        E.tensor_copy(t32[:, 512 * tc:512 * tc + 512],
                                      pMISC[32:33, 0:512]))(),
             inc=S["c1w"], drain=True)
    P.dma("gpsimd", rank_rt[1:2, :], t32[:, :],
          waits=[(S["c1w"], S["c1w"].n)], inc=S["c1d"])
    P.do("vector", lambda E: E.tensor_add(rank_rt, rank_rt, zp_rt),
         waits=[(S["c1w"], S["c1w"].n), (S["c1d"], S["c1d"].n)],
         inc=S["c1v"], drain=True)
    P.do("vector", lambda E: E.tensor_scalar(sel_rt, rank_rt, 1024.0, None, ALU.is_lt),
         inc=S["c1v"], drain=True)
    P.do("vector", lambda E: E.tensor_scalar(nmqbf[:, :], rank_rt[0:1, :], 1024.0,
                                             None, ALU.is_ge),
         inc=S["c1v"], drain=True)
    for c in range(16):
        P.do("tensor", (lambda c=c: lambda E:
                        E.transpose(pMISC[0:128, 0:2],
                                    sel_rt[:, 128 * c:128 * c + 128], idf[0:2, 0:2]))(),
             waits=[(S["c1v"], S["c1v"].n), (S["c1w"], S["c1w"].n)],
             inc=S["c1m"])
        P.do("vector", (lambda c=c: lambda E:
                        E.tensor_copy(mtok[:, 2 * c:2 * c + 2], pMISC[0:128, 0:2]))(),
             waits=[(S["c1m"], S["c1m"].n)], inc=S["c1w"], drain=True)
    c1stage = P.end_stage()
    PEND.update(S[k] for k in ("c1p", "c1v", "c1w", "c1s", "c1m", "c1x", "c1d",
                               "c1ov", "c1os", "c1og"))

    VC2 = S["vec"].n
    for it in range(128):
        h, wp = divmod(it, 16)
        hp, hr = divmod(h, 2)
        msel = 1 if wp == 0 else (2 if wp == 15 else 0)
        sl = it % 2
        P.do("tensor", (lambda hr=hr, hp=hp, wp=wp, sl=sl: lambda E:
                        E.matmul(pSIM[:, 1024 * sl:1024 * sl + 256],
                                 lq[64 * hr:64 * hr + 64,
                                    2048 * hp + 128 * wp:2048 * hp + 128 * wp + 128],
                                 lk[64 * hr:64 * hr + 64,
                                    2176 * hp + 128 * wp:2176 * hp + 128 * wp + 256],
                                 start=True, stop=True))(),
             waits=[(S["mx"], max(0, S["mx"].n - 1)), (S["vec"], VC2)],
             inc=S["sim"])
        P.do("vector", (lambda sl=sl, msel=msel: lambda E:
                        E.scalar_tensor_tensor(lsim[sl][:, :],
                                               pSIM[:, 1024 * sl:1024 * sl + 256],
                                               0.125,
                                               lmask[:, 256 * msel:256 * msel + 256],
                                               ALU.mult, ALU.add))(),
             waits=[(S["sim"], S["sim"].n), (S["ex"], max(0, S["ex"].n - 1))],
             inc=S["mx"], drain=True)
        P.do("vector", (lambda sl=sl: lambda E:
                        E.reduce_max(smwk[:, sl:sl + 1], lsim[sl][:, :], axis=AX.X,
                                     negate=True))(),
             inc=S["vec"], drain=True)
        P.do("scalar", (lambda sl=sl: lambda E:
                        E.activation(lattnT[sl][:, :], lsim[sl][:, :], AF.Exp,
                                     bias=smwk[:, sl:sl + 1], scale=1.0))(),
             waits=[(S["vec"], S["vec"].n), (S["tp"], max(0, S["tp"].n - 1))],
             inc=S["ex"], drain=True)
        P.do("vector", (lambda sl=sl: lambda E:
                        E.reduce_sum(smwk[:, 2 + sl:3 + sl], lattnT[sl][:, :],
                                     axis=AX.X))(),
             waits=[(S["ex"], S["ex"].n)], inc=S["vec"], drain=True)
        P.do("vector", (lambda sl=sl: lambda E:
                        E.reciprocal(smwk[:, 4 + sl:5 + sl], smwk[:, 2 + sl:3 + sl]))(),
             inc=S["vec"], drain=True)
        P.do("vector", (lambda sl=sl: lambda E:
                        E.tensor_scalar(lattnT[sl][:, :], lattnT[sl][:, :],
                                        smwk[:, 4 + sl:5 + sl], None, ALU.mult))(),
             inc=S["vec"], drain=True)
        def tp2(sl=sl):
            def f(E):
                E.transpose(pT[:, 0:128], lattnT[sl][:, 0:128], idbf[:, :])
                return E.transpose(pT[:, 128:256], lattnT[sl][:, 128:256], idbf[:, :])
            return f
        P.do("tensor", tp2(), waits=[(S["vec"], S["vec"].n), (S["tev"], S["tev"].n)],
             inc=S["tp"])
        P.do("vector", (lambda sl=sl: lambda E:
                        E.tensor_copy(lattnTT[sl][:, :], pT[:, 0:256]))(),
             waits=[(S["tp"], S["tp"].n)], inc=S["tev"], drain=True)
        def mkav(h=h, wp=wp, sl=sl):
            def f(E):
                ins = None
                for jc in range(2):
                    ins = E.matmul(pOUT[0:64, 0:128],
                                   vsh[:, 512 * (wp + jc) + 64 * h:
                                       512 * (wp + jc) + 64 * h + 64],
                                   lattnTT[sl][:, 128 * jc:128 * jc + 128],
                                   start=(jc == 0), stop=(jc == 1))
                return ins
            return f
        P.do("tensor", mkav(), waits=[(S["tev"], S["tev"].n), (S["oe"], S["oe"].n)],
             inc=S["av"])
        P.do("vector", (lambda hr=hr, hp=hp, wp=wp: lambda E:
                        E.tensor_copy(lightT[64 * hr:64 * hr + 64,
                                             2048 * hp + 128 * wp:
                                             2048 * hp + 128 * wp + 128],
                                      pOUT[0:64, 0:128]))(),
             waits=[(S["av"], S["av"].n)], inc=S["oe"], drain=True)
        P.feed(c1s1, (it + 1) / 40.0)
        P.feed(c1stage, (it - 39) / 80.0)
    P.feed(c1s1, 1.0)
    P.feed(c1stage, 1.0)
    assert not any(c1s1[1].values())
    assert not any(c1stage[1].values())
    PEND.clear()
    P.flush("c2")
    if "light" in debug_taps:
        tap("lightT", lightT)
    if "route" in debug_taps:
        tap("rank_rt", rank_rt); tap("mtok", mtok)
        tap("zp_rt", zp_rt)

    C1F = [(S[k], S[k].n) for k in ("c1p", "c1v", "c1w", "c1s", "c1m", "c1x",
                                    "c1d")]
    for _e in ("tensor", "vector", "scalar", "gpsimd"):
        P.do(_e, lambda E: E.nop(), waits=C1F)
    P.do("vector", lambda E: E.tensor_mul(t_row, r_row, sr_row),
         inc=S["vec"], drain=True)
    bcast(t_row, rsB)
    bcast(m_row, mB2)
    for dc in range(8):
        ld = xload(xs2[dc % 2][:, :], dc, [(S["vec"], max(0, S["vec"].n - 2))],
                   alt=True)
        P.do("vector", (lambda d=dc: lambda E:
                        E.tensor_sub(xs2[d % 2][:, :], xs2[d % 2][:, :], mB2[:, :]))(),
             waits=ld + [(S["scl"], S["scl"].n)], inc=S["vec"], drain=True)
        P.do("vector", (lambda d=dc: lambda E:
                        E.tensor_mul(xs2[d % 2][:, :], xs2[d % 2][:, :], rsB[:, :]))(),
             inc=S["vec"], drain=True)
        P.do("vector", (lambda d=dc: lambda E:
                        E.tensor_scalar(xrT[:, 2048 * d:2048 * d + 2048],
                                        xs2[d % 2][:, :],
                                        gam4[:, 16 + d:16 + d + 1], None, ALU.mult))(),
             inc=S["vec"], drain=True)
    P.flush("d0a")
    if "xrT" in debug_taps:
        tap("xrT", xrT)

    def rope_gen(wdr, dst, is_q, stride):
        for fc in range(4):
            wsem = S[f"sgs{fc % 2}"]
            P.dma("gpsimd", wst[fc % 2][:, :],
                  wdr[:, 128 * fc:128 * fc + 128]
                  .rearrange("(c p) f -> p c f", p=128),
                  waits=[(S["mm2"], max(0, S["mm2"].n - 4))], inc=wsem)
            sgn = wsem.n
            for tc in range(4):
                k = 4 * fc + tc
                def mk(fc=fc, tc=tc, k=k):
                    def f(E):
                        ins = None
                        for dc in range(8):
                            ins = E.matmul(pch(k % 2),
                                           wst[fc % 2][:, 128 * dc:128 * dc + 128],
                                           xrT[:, 2048 * dc + 512 * tc:
                                               2048 * dc + 512 * tc + 512],
                                           start=(dc == 0), stop=(dc == 7))
                        return ins
                    return f
                P.do("tensor", mk(),
                     waits=[(wsem, sgn), (S["vec"], S["vec"].n),
                            (S["vec2"], max(0, S["vec2"].n - 1))], inc=S["mm2"])
                sc0 = 0.125 if is_q else 1.0
                P.do("vector", (lambda k=k, tc=tc, sc0=sc0: lambda E:
                                E.scalar_tensor_tensor(
                                    ropea[:, :], pch(k % 2), sc0,
                                    cos2[:, 512 * tc:512 * tc + 512],
                                    ALU.mult, ALU.mult))(),
                     waits=[(S["mm2"], S["mm2"].n)], inc=S["vec3"], drain=True)
                for bb in range(4):
                    P.do("vector", (lambda k=k, tc=tc, bb=bb, sc0=sc0: lambda E:
                                    E.scalar_tensor_tensor(
                                        ropeb[32 * bb:32 * bb + 32, :],
                                        pch(k % 2)[32 * (bb ^ 1):32 * (bb ^ 1) + 32, :],
                                        sc0,
                                        sin2[32 * bb:32 * bb + 32,
                                             512 * tc:512 * tc + 512],
                                        ALU.mult, ALU.mult))(),
                         inc=S["vec3"], drain=True)
                dsl = dst[:, stride * fc + 512 * tc:stride * fc + 512 * tc + 512]
                P.do("vector", (lambda d=dsl: lambda E:
                                E.tensor_add(d, ropea[:, :], ropeb[:, :]))(),
                     inc=S["vec2"], drain=True)

    rope_gen(dr["whq"], hq, True, 2048)
    rope_gen(dr["wk_t"], hk, False, 2064)
    P.flush("d0qk")

    P.dma("gpsimd", wvfull[:, :], dr["wv_t"].rearrange("(c p) f -> p c f", p=128),
          waits=[(S["mm2"], S["mm2"].n)], inc=S["sgw"])
    for hp2 in range(4):
        P.dma("gpsimd", hk[:, 2064 * hp2 + 2048:2064 * hp2 + 2049],
              dr["nullk"][hp2, :].rearrange("(p o) -> p o", o=1),
              waits=[(S["vec2"], S["vec2"].n)], inc=S["sgn"])
    P.dma("gpsimd", hv[0:1, 520 * 16:520 * 16 + 520], dr["nullv"], inc=S["sgn"])
    sgv = S["sgw"].n
    sgnv = S["sgn"].n
    for tt in range(16):
        def mk(tt=tt):
            def f(E):
                ins = None
                for dc in range(8):
                    ins = E.matmul(pch(tt % 2),
                                   xrT[:, 2048 * dc + 128 * tt:2048 * dc + 128 * tt + 128],
                                   wvfull[:, 512 * dc:512 * dc + 512],
                                   start=(dc == 0), stop=(dc == 7))
                return ins
            return f
        P.do("tensor", mk(), waits=[(S["sgw"], sgv), (S["vec3"], S["vec3"].n),
                                    (S["vec"], max(0, S["vec"].n - 1))],
             inc=S["mm"])
        def mkev(tt=tt):
            def f(E):
                ins = None
                for h in range(8):
                    ins = E.tensor_scalar(
                        hv[:, 520 * tt + 65 * h:520 * tt + 65 * h + 64],
                        pch(tt % 2)[:, 64 * h:64 * h + 64],
                        mtok[:, 2 * tt + 1:2 * tt + 2], None, ALU.mult)
                return ins
            return f
        P.do("vector", mkev(), waits=[(S["mm"], S["mm"].n)], drain=True, inc=None)
        def mkcp(tt=tt):
            def f(E):
                ins = None
                for h in range(8):
                    ins = E.tensor_copy(
                        hv[:, 520 * tt + 65 * h + 64:520 * tt + 65 * h + 65],
                        mtok[:, 2 * tt + 1:2 * tt + 2])
                return ins
            return f
        P.do("vector", mkcp(), inc=S["vec"], drain=True)
    P.flush("d0v")
    if "hqkv" in debug_taps:
        tap("hq", hq)
        for _hp in range(4):
            tap(f"hk{_hp}", hk[:, 2064 * _hp:2064 * _hp + 2049])
        tap("hv", hv[:, 0:8320]); tap("hvn", hv[0:1, 8320:8840])

    P.do("vector", lambda E: E.tensor_scalar(selqbf32[:, :], rank_rt[0:1, :], 1024.0,
                                             None, ALU.is_lt),
         waits=[(S["vec2"], S["vec2"].n)], inc=S["vec"], drain=True)
    B = {k: S[k].n for k in ("sim", "mx", "ex", "tp", "tes", "tev", "av",
                             "rr", "rb", "oe")}
    VD0 = S["vec"].n
    SGN0 = S["sgn"].n
    tailT, tailV, tailH = [], [], [None]
    for it in range(128):
        h, qc = divmod(it, 16)
        hp, hr = divmod(h, 2)
        qsl = hq[64 * hr:64 * hr + 64, 2048 * hp + 128 * qc:2048 * hp + 128 * qc + 128]

        for kc in range(4):
            P.do("tensor", (lambda kc=kc, qsl=qsl, hp=hp, hr=hr: lambda E:
                            E.matmul(pSIM[:, 512 * kc:512 * kc + 512], qsl,
                                     hk[64 * hr:64 * hr + 64,
                                        2064 * hp + 512 * kc:2064 * hp + 512 * kc + 512],
                                     start=True, stop=True))(),
                 waits=[(S["ex"], B["ex"] + max(0, 3 * it - (2 if kc < 2 else 1))),
                        (S["vec"], VD0), (S["sgn"], SGN0)],
                 inc=S["sim"])
        P.do("tensor", (lambda qsl=qsl, hp=hp, hr=hr: lambda E:
                        E.matmul(pMISC[:, 384:385], qsl,
                                 hk[64 * hr:64 * hr + 64,
                                    2064 * hp + 2048:2064 * hp + 2049],
                                 start=True, stop=True))(),
             waits=[(S["ex"], B["ex"] + max(0, 3 * it)), (S["rb"], S["rb"].n)],
             inc=S["sim"])
        for th in tailT:
            th()
        tailT = []
        for kc in range(4):
            P.do("vector", (lambda kc=kc: lambda E:
                            E.reduce_max(smwk[:, 2 + kc:3 + kc],
                                         pSIM[:, 512 * kc:512 * kc + 512],
                                         axis=AX.X))(),
                 waits=[(S["sim"], B["sim"] + 5 * it + kc + 1),
                        (S["ex"], B["ex"] + max(0, 3 * (it - 1)))],
                 drain=True)
        P.do("vector", lambda E: E.tensor_copy(smwk[:, 6:7], pMISC[:, 384:385]),
             waits=[(S["sim"], B["sim"] + 5 * it + 5)], drain=True)
        P.do("vector", (lambda sl=it % 2: lambda E:
                        E.reduce_max(smwk[:, sl:sl + 1], smwk[:, 2:7], axis=AX.X,
                                     negate=True))(),
             inc=S["mx"], drain=True)
        for th in tailV:
            th()
        tailV = []
        if tailH[0] is not None:
            tailH[0]()
            tailH[0] = None
        oeW = S["oe"].n
        P.do("scalar", (lambda sl=it % 2: lambda E:
                        E.activation(attnbf[:, 0:1024], pSIM[:, 0:1024], AF.Exp,
                                     bias=smwk[:, sl:sl + 1], scale=1.0))(),
             waits=[(S["mx"], B["mx"] + it + 1), (S["tp"], B["tp"] + 3 * it)],
             inc=S["ex"], drain=True)
        P.do("scalar", (lambda sl=it % 2: lambda E:
                        E.activation(attnbf[:, 1024:2048], pSIM[:, 1024:2048], AF.Exp,
                                     bias=smwk[:, sl:sl + 1], scale=1.0))(),
             inc=S["ex"], drain=True)
        P.do("scalar", (lambda sl=it % 2: lambda E:
                        E.activation(attnbf[:, 2048:2049], pMISC[:, 384:385], AF.Exp,
                                     bias=smwk[:, sl:sl + 1], scale=1.0))(),
             inc=S["ex"], drain=True)
        P.do("scalar", lambda E: E.copy(attnT[:, 0:1024], pT[:, 0:1024]),
             waits=[(S["tp"], B["tp"] + 3 * it + 1)], inc=S["tes"], drain=True)
        P.do("scalar", lambda E: E.copy(attnT[:, 1024:2048], pT[:, 1024:2048]),
             waits=[(S["tp"], B["tp"] + 3 * it + 2)], inc=S["tes"], drain=True)
        def tpb(j0, half):
            def f(E):
                ins = None
                for j in range(j0, j0 + 8):
                    ins = E.transpose(pT[:, 1024 * half + 128 * (j - j0):
                                         1024 * half + 128 * (j - j0) + 128],
                                      attnbf[:, 128 * j:128 * j + 128], idbf[:, :])
                return ins
            return f
        def av(j0, j1, h=h, start=False):
            def f(E):
                ins = None
                for j in range(j0, j1):
                    ins = E.matmul(pOUT[0:65, 0:128],
                                   hv[:, 520 * j + 65 * h:520 * j + 65 * h + 65],
                                   attnT[:, 128 * j:128 * j + 128],
                                   start=(start and j == j0), stop=False)
                return ins
            return f
        tailT = [
            (lambda it=it: P.do("tensor", tpb(0, 0),
                 waits=[(S["ex"], B["ex"] + 3 * it + 1),
                        (S["tes"], B["tes"] + max(0, 2 * it - 1)),
                        (S["tev"], B["tev"] + it)], inc=S["tp"])),
            (lambda it=it: P.do("tensor", tpb(8, 1),
                 waits=[(S["ex"], B["ex"] + 3 * it + 2),
                        (S["tes"], B["tes"] + 2 * it)], inc=S["tp"])),
            (lambda it=it: P.do("tensor", (lambda: lambda E:
                 E.transpose(pT[0:1, 0:128], attnbf[:, 2048:2049], idbf[:, :]))(),
                 waits=[(S["ex"], B["ex"] + 3 * it + 3),
                        (S["tes"], B["tes"] + 2 * it + 1)], inc=S["tp"])),
            (lambda it=it, oeW=oeW: P.do("tensor", av(0, 8, start=True),
                 waits=[(S["oe"], oeW), (S["tes"], B["tes"] + 2 * it + 1)])),
            (lambda it=it: P.do("tensor", av(8, 16),
                 waits=[(S["tes"], B["tes"] + 2 * it + 2)])),
            (lambda it=it, h=h: P.do("tensor", (lambda h=h: lambda E:
                 E.matmul(pOUT[0:65, 0:128],
                          hv[0:1, 520 * 16 + 65 * h:520 * 16 + 65 * h + 65],
                          attnT[0:1, 2048:2176], start=False, stop=True))(),
                 waits=[(S["tev"], B["tev"] + it + 1)], inc=S["av"])),
        ]
        tailV = [
            (lambda it=it: P.do("vector", lambda E:
                 E.tensor_copy(attnT[0:1, 2048:2176], pT[0:1, 0:128]),
                 waits=[(S["tp"], B["tp"] + 3 * it + 3)], inc=S["tev"], drain=True)),
            (lambda it=it, qc=qc: P.do("vector", (lambda qc=qc: lambda E:
                 E.tensor_copy(denrow[0:1, 128 * qc:128 * qc + 128],
                               pOUT[64:65, 0:128]))(),
                 waits=[(S["av"], B["av"] + it + 1)], drain=True)),
            (lambda it=it, hr=hr, hp=hp, qc=qc: P.do("vector",
                 (lambda hr=hr, hp=hp, qc=qc: lambda E:
                  E.tensor_copy(heavyT[64 * hr:64 * hr + 64,
                                       2048 * hp + 128 * qc:
                                       2048 * hp + 128 * qc + 128],
                                pOUT[0:64, 0:128]))(),
                 inc=S["oe"], drain=True)),
        ]
        if qc == 15:
            def headend(h=h, hr=hr, hp=hp, it=it):
                P.do("vector", lambda E: E.reciprocal(denrow[0:1, :],
                                                      denrow[0:1, :]), drain=True)
                P.do("vector", (lambda: lambda E:
                                E.tensor_tensor(denrow[0:1, :], denrow[0:1, :],
                                                selqbf32[0:1, :], ALU.mult))(),
                     inc=S["rr"], drain=True)
                for tcq in range(8):
                    P.do("tensor", (lambda tcq=tcq: lambda E:
                                    E.matmul(pMISC[0:128, 0:256], ones_f[0:1, :],
                                             denrow[0:1, 256 * tcq:256 * tcq + 256],
                                             start=True, stop=True))(),
                         waits=[(S["rr"], S["rr"].n), (S["rb"], S["rb"].n),
                                (S["ex"], min(B["ex"] + 3 * (it + 1) + 3,
                                              B["ex"] + 3 * 128)),
                                (S["mx"], min(B["mx"] + it + 2,
                                              B["mx"] + 128))],
                         inc=S["mm3"])
                    P.do("vector", (lambda tcq=tcq: lambda E:
                                    E.tensor_copy(rbig[:, 256 * tcq:256 * tcq + 256],
                                                  pMISC[0:128, 0:256]))(),
                         waits=[(S["mm3"], S["mm3"].n)], inc=S["rb"], drain=True)
                P.do("vector", (lambda hr=hr, hp=hp: lambda E:
                                E.tensor_mul(heavyT[64 * hr:64 * hr + 64,
                                                    2048 * hp:2048 * hp + 2048],
                                             heavyT[64 * hr:64 * hr + 64,
                                                    2048 * hp:2048 * hp + 2048],
                                             rbig[64 * hr:64 * hr + 64, :]))(),
                     inc=S["oe"], drain=True)
            tailH[0] = headend
    for th in tailT:
        th()
    for th in tailV:
        th()
    if tailH[0] is not None:
        tailH[0]()
    P.flush("d")
    if "heavy" in debug_taps:
        tap("heavyT", heavyT)

    P.dma("gpsimd", wlofull[:, :], dr["wlo"].rearrange("(c p) f -> p c f", p=128),
          waits=[(S["mm"], S["mm"].n), (S["mm2"], S["mm2"].n),
                 (S["sgw"], S["sgw"].n)], inc=S["sgw"])
    P.dma("gpsimd", whofull[:, :], dr["whout"].rearrange("(c p) f -> p c f", p=128),
          waits=[(S["ex"], S["ex"].n)], inc=S["sgw"])
    sgE = S["sgw"].n
    for dc in range(8):
        ld = xload(xs2[dc % 2][:, :], dc, [(S["vec"], max(0, S["vec"].n - 4))])
        for tc in range(4):
            k = 4 * dc + tc
            def mk(dc=dc, tc=tc, k=k):
                def f(E):
                    p = pch(k % 2)
                    for fc in range(4):
                        E.matmul(p, wlofull[:, 1024 * fc + 128 * dc:
                                            1024 * fc + 128 * dc + 128],
                                 lightT[:, 2048 * fc + 512 * tc:
                                        2048 * fc + 512 * tc + 512],
                                 start=(fc == 0), stop=False)
                    for fc in range(4):
                        E.matmul(p, whofull[:, 1024 * fc + 128 * dc:
                                            1024 * fc + 128 * dc + 128],
                                 heavyT[:, 2048 * fc + 512 * tc:
                                        2048 * fc + 512 * tc + 512],
                                 start=False, stop=False)
                    return E.matmul(p, nqbf[:, 128 * dc:128 * dc + 128],
                                    nmqbf[:, 512 * tc:512 * tc + 512],
                                    start=False, stop=True)
                return f
            P.do("tensor", mk(),
                 waits=[(S["sgw"], sgE)] + ld + [(S["oe"], S["oe"].n),
                        (S["vec"], max(0, S["vec"].n - 1))],
                 inc=S["mm"])
            P.do("vector", (lambda dc=dc, tc=tc, k=k: lambda E:
                            E.scalar_tensor_tensor(
                                x1[:, 2048 * dc + 512 * tc:2048 * dc + 512 * tc + 512],
                                pch(k % 2), 1.0,
                                xs2[dc % 2][:, 512 * tc:512 * tc + 512],
                                ALU.mult, ALU.add))(),
                 waits=[(S["mm"], S["mm"].n)], inc=S["vec"], drain=True)
    P.flush("e")
    if "x1" in debug_taps:
        tap("x1", x1)

    for dc in range(8):
        P.do("scalar", (lambda d=dc: lambda E:
                        E.activation(fsq[:, :], x1[:, 2048 * d:2048 * d + 2048],
                                     AF.Square))(),
             waits=[(S["mm"], S["mm"].n), (S["vec"], S["vec"].n)],
             inc=S["scl"], drain=True)
        def mk(d=dc):
            def f(E):
                ins = None
                for tc in range(4):
                    ins = E.matmul(pSIM[0:1, 512 * tc:512 * tc + 512], ones_f[:, 0:1],
                                   x1[:, 2048 * d + 512 * tc:2048 * d + 512 * tc + 512],
                                   start=(d == 0), stop=(d == 7), skip_group_check=True)
                for tc in range(4):
                    ins = E.matmul(pSIM[32:33, 512 * tc:512 * tc + 512], ones_f[:, 1:2],
                                   fsq[:, 512 * tc:512 * tc + 512],
                                   start=(d == 0), stop=(d == 7), skip_group_check=True)
                return ins
            return f
        P.do("tensor", mk(), waits=[(S["scl"], S["scl"].n)], inc=S["mm"])
    row_stats(m_row, r_row)
    bcast(m_row, m3B)
    bcast(r_row, r3B)
    for dc in range(8):
        P.dma("gpsimd", fsq[:, 8 * dc:8 * dc + 8],
              dr["wg_t"][128 * dc:128 * dc + 128, :],
              waits=[(S["mm"], S["mm"].n)], inc=S["sgf"])
    def gsum(E):
        ins = None
        for dc in range(8):
            ins = E.matmul(pMISC[0:8, 0:1], fsq[:, 8 * dc:8 * dc + 8],
                           ones_f[:, 0:1], start=(dc == 0), stop=(dc == 7))
        return ins
    P.do("tensor", gsum, waits=[(S["sgf"], S["sgf"].n), (S["scl"], S["scl"].n)],
         inc=S["mm"])
    def glog(E):
        ins = None
        for dc in range(8):
            for tc in range(4):
                ins = E.matmul(pSIM[64:72, 512 * tc:512 * tc + 512],
                               fsq[:, 8 * dc:8 * dc + 8],
                               x1[:, 2048 * dc + 512 * tc:2048 * dc + 512 * tc + 512],
                               start=(dc == 0), stop=(dc == 7), skip_group_check=True)
        return ins
    P.do("tensor", glog, inc=S["mm"])
    P.do("vector", lambda E: E.tensor_copy(wgsum[:, 0:1], pMISC[0:8, 0:1]),
         waits=[(S["mm"], S["mm"].n)], inc=S["vec"], drain=True)
    P.do("vector", lambda E: E.tensor_copy(lgsb, pSIM[64:72, 0:2048]),
         inc=S["vec"], drain=True)
    P.do("vector", lambda E: E.tensor_copy(mr8, m3B[0:8, :]), inc=S["vec"], drain=True)
    P.do("vector", lambda E: E.tensor_copy(rr8, r3B[0:8, :]), inc=S["vec"], drain=True)
    P.do("vector", lambda E: E.tensor_scalar(mr8, mr8, wgsum[:, 0:1], None, ALU.mult),
         inc=S["vec"], drain=True)
    P.do("vector", lambda E: E.tensor_sub(lgsb, lgsb, mr8), inc=S["vec"], drain=True)
    P.do("vector", lambda E: E.tensor_mul(lgsb, lgsb, rr8), inc=S["vec"], drain=True)
    for c in range(16):
        P.do("tensor", (lambda c=c: lambda E:
                        E.transpose(pMISC[0:128, 0:8], lgsb[:, 128 * c:128 * c + 128],
                                    idf[0:8, 0:8]))(),
             waits=[(S["vec"], S["vec"].n), (S["vec2"], S["vec2"].n)], inc=S["mm"])
        P.do("vector", lambda E: E.tensor_copy(gtok[:, 0:8], pMISC[0:128, 0:8]),
             waits=[(S["mm"], S["mm"].n)], inc=S["vec"], drain=True)
        P.do("vector", lambda E: E.reduce_max(gtok[:, 8:9], gtok[:, 0:8], axis=AX.X,
                                              negate=True),
             inc=S["vec"], drain=True)
        P.do("scalar", lambda E: E.activation(gtok[:, 24:32], gtok[:, 0:8], AF.Exp,
                                              bias=gtok[:, 8:9], scale=1.0),
             waits=[(S["vec"], S["vec"].n)], inc=S["scl"], drain=True)
        P.do("vector", lambda E: E.reduce_sum(gtok[:, 9:10], gtok[:, 24:32], axis=AX.X),
             waits=[(S["scl"], S["scl"].n)], inc=S["vec"], drain=True)
        P.do("vector", lambda E: E.reciprocal(gtok[:, 10:11], gtok[:, 9:10]),
             inc=S["vec"], drain=True)
        P.do("vector", lambda E: E.tensor_scalar(gtok[:, 24:32], gtok[:, 24:32],
                                                 gtok[:, 10:11], None, ALU.mult),
             inc=S["vec"], drain=True)
        P.do("vector", lambda E: E.reduce_max(gtok[:, 11:12], gtok[:, 24:32], axis=AX.X),
             inc=S["vec"], drain=True)
        P.do("vector", lambda E: E.tensor_scalar(gtok[:, 32:40], gtok[:, 24:32],
                                                 gtok[:, 11:12], None, ALU.is_equal),
             inc=S["vec"], drain=True)
        P.do("vector", lambda E: E.scalar_tensor_tensor(gtok[:, 40:48], gtok[:, 32:40],
                                                        1.0, im99[:, :],
                                                        ALU.mult, ALU.mult),
             inc=S["vec"], drain=True)
        P.do("vector", lambda E: E.tensor_scalar(gtok[:, 40:48], gtok[:, 40:48], 99.0,
                                                 None, ALU.add),
             inc=S["vec"], drain=True)
        P.do("vector", lambda E: E.tensor_reduce(gtok[:, 12:13], gtok[:, 40:48],
                                                 axis=AX.X, op=ALU.min),
             inc=S["vec"], drain=True)
        P.do("vector", lambda E: E.tensor_scalar(gtok[:, 32:40], iota8[:, :],
                                                 gtok[:, 12:13], None, ALU.is_equal),
             inc=S["vec"], drain=True)
        P.do("vector", lambda E: E.scalar_tensor_tensor(gtok[:, 48:56], gtok[:, 32:40],
                                                        -2.0, gtok[:, 24:32],
                                                        ALU.mult, ALU.add),
             inc=S["vec"], drain=True)
        P.do("vector", lambda E: E.reduce_max(gtok[:, 13:14], gtok[:, 48:56], axis=AX.X),
             inc=S["vec"], drain=True)
        P.do("vector", lambda E: E.tensor_scalar(gtok[:, 48:56], gtok[:, 48:56],
                                                 gtok[:, 13:14], None, ALU.is_equal),
             inc=S["vec"], drain=True)
        P.do("vector", lambda E: E.scalar_tensor_tensor(gtok[:, 48:56], gtok[:, 48:56],
                                                        1.0, im99[:, :],
                                                        ALU.mult, ALU.mult),
             inc=S["vec"], drain=True)
        P.do("vector", lambda E: E.tensor_scalar(gtok[:, 48:56], gtok[:, 48:56], 99.0,
                                                 None, ALU.add),
             inc=S["vec"], drain=True)
        P.do("vector", lambda E: E.tensor_reduce(gtok[:, 14:15], gtok[:, 48:56],
                                                 axis=AX.X, op=ALU.min),
             inc=S["vec"], drain=True)
        P.do("vector", lambda E: E.tensor_scalar(gtok[:, 40:48], iota8[:, :],
                                                 gtok[:, 14:15], None, ALU.is_equal),
             inc=S["vec"], drain=True)
        P.do("vector", lambda E: E.tensor_add(gtok[:, 15:16], gtok[:, 11:12],
                                              gtok[:, 13:14]),
             inc=S["vec"], drain=True)
        P.do("vector", lambda E: E.reciprocal(gtok[:, 15:16], gtok[:, 15:16]),
             inc=S["vec"], drain=True)
        P.do("vector", lambda E: E.tensor_scalar(gtok[:, 32:40], gtok[:, 32:40],
                                                 gtok[:, 11:12], gtok[:, 15:16],
                                                 ALU.mult, ALU.mult),
             inc=S["vec"], drain=True)
        P.do("vector", lambda E: E.tensor_scalar(gtok[:, 40:48], gtok[:, 40:48],
                                                 gtok[:, 13:14], gtok[:, 15:16],
                                                 ALU.mult, ALU.mult),
             inc=S["vec"], drain=True)
        P.do("vector", lambda E: E.tensor_add(gtok[:, 0:8], gtok[:, 32:40],
                                              gtok[:, 40:48]),
             inc=S["vec"], drain=True)
        P.do("tensor", (lambda c=c: lambda E:
                        E.transpose(pMISC[0:8, 0:128], gtok[:, 0:8], idf[:, :]))(),
             waits=[(S["vec"], S["vec"].n)], inc=S["mm2"])
        P.do("vector", (lambda c=c: lambda E:
                        E.tensor_copy(gatesF[:, 128 * c:128 * c + 128],
                                      pMISC[0:8, 0:128]))(),
             waits=[(S["mm2"], S["mm2"].n)], inc=S["vec2"], drain=True)
    for dc in range(8):
        for tc in range(4):
            k = 4 * dc + tc
            st = mrst[k % 2]
            P.do("vector", (lambda dc=dc, tc=tc, st=st: lambda E:
                            E.scalar_tensor_tensor(
                                st[:, :],
                                x1[:, 2048 * dc + 512 * tc:2048 * dc + 512 * tc + 512],
                                1.0, m3B[:, 512 * tc:512 * tc + 512],
                                ALU.mult, ALU.subtract))(),
                 waits=[(S["scl2"], max(0, S["scl2"].n - 1))], inc=S["vec"], drain=True)
            P.do("vector", (lambda tc=tc, st=st: lambda E:
                            E.tensor_mul(st[:, :], st[:, :],
                                         r3B[:, 512 * tc:512 * tc + 512]))(),
                 inc=S["vec"], drain=True)
            P.do("scalar", (lambda dc=dc, tc=tc, st=st: lambda E:
                            E.copy(xl2T[:, 2048 * dc + 512 * tc:
                                        2048 * dc + 512 * tc + 512], st[:, :]))(),
                 waits=[(S["vec"], S["vec"].n)], inc=S["scl2"], drain=True)
    P.flush("f")
    if "gates" in debug_taps:
        tap("gatesF", gatesF)
    if "xl2T" in debug_taps:
        tap("xl2T", xl2T)

    EQ = [(e, q) for e in range(8) for q in range(4)]
    sg_at = {}

    def wdma(eq_i, extra=()):
        e, q = EQ[eq_i]
        par = eq_i % 2
        sem = S[f"sgp{par}"]
        w = list(extra) + [(S["mm4"], max(0, eq_i - 1))]
        P.dma("gpsimd", w1b[par][:, :],
              dr["w1q"][e, q].rearrange("(c p) f -> p c f", p=128),
              waits=w, inc=sem)
        P.dma("gpsimd", w3b[par][:, :],
              dr["w3q"][e, q].rearrange("(c p) f -> p c f", p=128), inc=sem)
        P.dma("gpsimd", w2b[par][:, :],
              dr["w2q"][e, q].rearrange("(c p) f -> p c f", p=128), inc=sem)
        P.dma("gpsimd", b1tqb[par][:, :], dr["b1tq"][e, q], inc=sem)
        P.dma("gpsimd", b3tqb[par][:, :], dr["b3tq"][e, q], inc=sem)
        P.dma("gpsimd", a2tqb[par][:, :],
              dr["a2tq"][e, q].rearrange("(c p) f -> p c f", p=128), inc=sem)
        if q == 0:
            P.dma("gpsimd", a1te[:, :],
                  dr["a1t"][e].rearrange("(c p) f -> p c f", p=128),
                  waits=[(S["mm4"], eq_i)], inc=sem)
            P.dma("gpsimd", a3te[:, :],
                  dr["a3t"][e].rearrange("(c p) f -> p c f", p=128), inc=sem)
            P.dma("gpsimd", b2te[:, :], dr["b2t"][e], inc=sem)
            P.dma("gpsimd", b1ce[:, :], dr["b1c"][e], inc=sem)
            P.dma("gpsimd", b3ce[:, :], dr["b3c"][e], inc=sem)
        sg_at[eq_i] = (sem, sem.n)

    wdma(0, extra=[(S["scl2"], S["scl2"].n), (S["vec2"], S["vec2"].n),
                   (S["vec"], S["vec"].n)])
    for eq_i in range(32):
        e, q = EQ[eq_i]
        par = eq_i % 2
        if eq_i + 1 < 32:
            wdma(eq_i + 1)
        if q == 0:
            P.do("vector", lambda E: E.memset(u2sb, 0.0),
                 waits=[(S["mm2"], S["mm2"].n)], inc=S["vec3"], drain=True)
        for tc in range(4):
            if q == 0:
                def mku(att, tc=tc):
                    def f(E):
                        ins = None
                        for dc in range(8):
                            ins = E.matmul(pMISC[0:16, 0:512],
                                           att[:, 16 * dc:16 * dc + 16],
                                           xl2T[:, 2048 * dc + 512 * tc:
                                                2048 * dc + 512 * tc + 512],
                                           start=(dc == 0), stop=(dc == 7))
                        return ins
                    return f
                P.do("tensor", mku(a1te),
                     waits=[sg_at[eq_i], (S["scl2"], S["scl2"].n),
                            (S["vec3"], S["vec3"].n), (S["vec"], S["vec"].n),
                            (S["sim"], S["sim"].n)],
                     inc=S["mm"])
                P.do("vector", (lambda tc=tc: lambda E:
                                E.tensor_copy(u1sb[:, 512 * tc:512 * tc + 512],
                                              pMISC[0:16, 0:512]))(),
                     waits=[(S["mm"], S["mm"].n)], inc=S["vec"], drain=True)
                P.do("tensor", mku(a3te), waits=[(S["vec"], S["vec"].n)], inc=S["mm"])
                P.do("vector", (lambda tc=tc: lambda E:
                                E.tensor_copy(u3sb[:, 512 * tc:512 * tc + 512],
                                              pMISC[0:16, 0:512]))(),
                     waits=[(S["mm"], S["mm"].n)], inc=S["vec"], drain=True)
            if q == 0:
                P.do("tensor", (lambda e=e, tc=tc: lambda E:
                                E.matmul(pch(3), sel8[:, 128 * e:128 * e + 128],
                                         gatesF[:, 512 * tc:512 * tc + 512],
                                         start=True, stop=True))(),
                     waits=[(S["tev"], S["tev"].n), (S["vec2"], S["vec2"].n),
                            (S["oe"], S["oe"].n), (S["scl2"], S["scl2"].n)],
                     inc=S["rb"])
                P.do("vector", (lambda tc=tc: lambda E:
                                E.tensor_copy(gateb[:, 512 * tc:512 * tc + 512],
                                              pch(3)))(),
                     waits=[(S["rb"], S["rb"].n), (S["vec"], S["vec"].n)],
                     inc=S["tev"], drain=True)
            UTC = S["vec"].n
            for hc in range(4):
                hcg = 4 * q + hc
                b1bank = 2 * (hc % 2)
                b3bank = 1 + 2 * (hc % 2)
                def h13(wb, btq, usb, bank, tc=tc, hc=hc):
                    def f(E):
                        for dc in range(8):
                            E.matmul(pSIM[:, 512 * bank:512 * bank + 512],
                                     wb[:, 512 * dc + 128 * hc:512 * dc + 128 * hc + 128],
                                     xl2T[:, 2048 * dc + 512 * tc:
                                          2048 * dc + 512 * tc + 512],
                                     start=(dc == 0), stop=False)
                        return E.matmul(pSIM[:, 512 * bank:512 * bank + 512],
                                        btq[:, 128 * hc:128 * hc + 128],
                                        usb[:, 512 * tc:512 * tc + 512],
                                        start=False, stop=True)
                    return f
                P.do("tensor", h13(w1b[par], b1tqb[par], u1sb, b1bank),
                     waits=[sg_at[eq_i], (S["vec"], UTC),
                            (S["scl"], max(0, S["scl"].n - 1)),
                            (S["oe"], max(0, S["oe"].n - 1))],
                     inc=S["mm"])
                P.do("scalar", (lambda hcg=hcg, b=b1bank: lambda E:
                                E.activation(gtmp1[:, :], pSIM[:, 512 * b:512 * b + 512],
                                             AF.Silu,
                                             bias=b1ce[:, hcg:hcg + 1], scale=1.0))(),
                     waits=[(S["mm"], S["mm"].n), (S["vec"], S["vec"].n)],
                     inc=S["scl"], drain=True)
                P.do("tensor", h13(w3b[par], b3tqb[par], u3sb, b3bank),
                     waits=[(S["scl2"], max(0, S["scl2"].n - 1)),
                            (S["tev"], S["tev"].n),
                            (S["oe"], S["oe"].n)], inc=S["sim"])
                P.do("scalar", (lambda hcg=hcg, b=b3bank: lambda E:
                                E.activation(gtmp2[:, :], pSIM[:, 512 * b:512 * b + 512],
                                             AF.Identity,
                                             bias=b3ce[:, hcg:hcg + 1], scale=1.0))(),
                     waits=[(S["sim"], S["sim"].n), (S["vec"], S["vec"].n)],
                     inc=S["scl2"], drain=True)
                P.do("vector", lambda E: E.tensor_mul(gtmp1[:, :], gtmp1[:, :],
                                                      gtmp2[:, :]),
                     waits=[(S["scl"], S["scl"].n), (S["scl2"], S["scl2"].n),
                            (S["mm2"], S["mm2"].n), (S["av"], S["av"].n)],
                     drain=True)
                P.do("vector", (lambda hc=hc, tc=tc: lambda E:
                                E.tensor_mul(gTq[:, 512 * hc:512 * hc + 512],
                                             gtmp1[:, :],
                                             gateb[:, 512 * tc:512 * tc + 512]))(),
                     inc=S["vec"], drain=True)
            def u2mm(par):
                def f(E):
                    ins = None
                    for hc in range(4):
                        ins = E.matmul(pMISC[32:48, 0:512],
                                       a2tqb[par][:, 16 * hc:16 * hc + 16],
                                       gTq[:, 512 * hc:512 * hc + 512],
                                       start=(hc == 0), stop=(hc == 3))
                    return ins
                return f
            GTC = S["vec"].n

            def emit_u2(tc=tc, par=par, q=q):
                P.do("tensor", u2mm(par), waits=[(S["vec"], GTC)], inc=S["mm2"])
                P.do("vector", (lambda tc=tc: lambda E:
                                E.scalar_tensor_tensor(
                                    u2sb[:, 512 * tc:512 * tc + 512],
                                    pMISC[32:48, 0:512], 1.0,
                                    u2sb[:, 512 * tc:512 * tc + 512],
                                    ALU.mult, ALU.add))(),
                     waits=[(S["mm2"], S["mm2"].n)], inc=S["vec3"], drain=True)
                if q == 3:
                    P.do("vector", (lambda tc=tc: lambda E:
                                    E.tensor_copy(u2bf[:, 512 * tc:512 * tc + 512],
                                                  u2sb[:, 512 * tc:512 * tc + 512]))(),
                         inc=S["vec3"], drain=True)

            if q == 3:
                emit_u2()
            for dc in range(8):
                ybank = 0 if dc % 2 == 0 else 1
                def ymm(dc=dc, tc=tc, par=par, q=q, eq_i=eq_i, yb=ybank):
                    def f(E):
                        ins = None
                        for hc in range(4):
                            last = (hc == 3) and q != 3 and eq_i != 3
                            ins = E.matmul(pch(yb),
                                           w2b[par][:, 1024 * hc + 128 * dc:
                                                    1024 * hc + 128 * dc + 128],
                                           gTq[:, 512 * hc:512 * hc + 512],
                                           start=(hc == 0), stop=last)
                        if q == 3:
                            ins = E.matmul(pch(yb),
                                           b2te[:, 128 * dc:128 * dc + 128],
                                           u2bf[:, 512 * tc:512 * tc + 512],
                                           start=False, stop=(eq_i != 3))
                        if eq_i == 3:
                            ins = E.matmul(pch(yb),
                                           b2allb[:, 128 * dc:128 * dc + 128],
                                           gatesF[:, 512 * tc:512 * tc + 512],
                                           start=False, stop=True)
                        return ins
                    return f
                yw = [(S["vec"], GTC), (S["oe"], max(0, S["oe"].n - 1)),
                      (S["scl"], S["scl"].n), (S["scl2"], S["scl2"].n)]
                if q == 3:
                    yw.append((S["vec3"], S["vec3"].n))
                P.do("tensor", ymm(), waits=yw, inc=S["av"])
                P.do("vector", (lambda dc=dc, tc=tc, yb=ybank: lambda E:
                                E.scalar_tensor_tensor(
                                    x1[:, 2048 * dc + 512 * tc:
                                       2048 * dc + 512 * tc + 512],
                                    pch(yb), 1.0,
                                    x1[:, 2048 * dc + 512 * tc:
                                       2048 * dc + 512 * tc + 512],
                                    ALU.mult, ALU.add))(),
                     waits=[(S["av"], S["av"].n)], inc=S["oe"], drain=True)
            if q != 3:
                emit_u2()
        P.do("tensor", lambda E: E.nop(),
             waits=[(S["scl"], S["scl"].n), (S["scl2"], S["scl2"].n),
                    (S["mm2"], S["mm2"].n), (S["av"], S["av"].n)],
             inc=S["mm4"])
    P.flush("g")

    for dc in range(8):
        P.dma("sync", out_d[128 * dc:128 * dc + 128, :],
              x1[:, 2048 * dc:2048 * dc + 2048],
              waits=[(S["oe"], S["oe"].n)], inc=S["fin"])
    P.do("sync", lambda E: E.nop(), waits=[(S["fin"], S["fin"].n)])
    P.flush("out")

    nc.finalize()
    return nc, sorted(TAPS.keys())


_BUILD_CACHE = {}


def kernel(**inputs):
    if "nc" not in _BUILD_CACHE:
        _BUILD_CACHE["nc"] = build()
    nc, _ = _BUILD_CACHE["nc"]
    CON = _host_constants()
    L = _host_layout(inputs)
    xTs = L.pop("_xT")
    static = {k: np.ascontiguousarray(v) for k, v in L.items()}
    for k, v in CON.items():
        static[k] = np.ascontiguousarray(v)
    in_maps = []
    for b in range(8):
        m = dict(static)
        m["xT"] = xTs[b]
        in_maps.append(m)
    res = run_bass_kernel_spmd(nc, in_maps, core_ids=list(range(8)))
    out = np.stack([np.ascontiguousarray(np.asarray(r["out"]).T)
                    for r in res.results])
    return out


# version 17
# speedup vs baseline: 1.0061x; 1.0061x over previous
import math
import numpy as np
import ml_dtypes
import concourse.bass as bass
import concourse.mybir as mybir
from concourse.bass_utils import run_bass_kernel_spmd

F32 = mybir.dt.float32
BF16 = mybir.dt.bfloat16
AF = mybir.ActivationFunctionType
ALU = mybir.AluOpType
AX = mybir.AxisListType
NEG = -3.4e38
BF = ml_dtypes.bfloat16
ENGS = ("tensor", "vector", "scalar", "gpsimd", "sync")


def _host_constants():
    c = {}
    invf = 1.0 / 10000.0 ** (np.arange(0, 64, 2, dtype=np.float64) / 64)
    pos = np.arange(2048, dtype=np.float64)[:, None] * invf[None]
    pos = np.concatenate([pos, pos], axis=1)
    cosT = np.cos(pos).T.astype(np.float32)
    sinT = np.sin(pos).T.astype(np.float32)
    c["cos2"] = np.ascontiguousarray(np.vstack([cosT, cosT]))
    sa = np.vstack([-sinT[0:32], sinT[32:64]])
    c["sin2"] = np.ascontiguousarray(np.vstack([sa, sa]))
    i_idx = np.arange(128)[:, None]
    j_idx = np.arange(256)[None, :]
    basev = (j_idx - i_idx >= 0) & (j_idx - i_idx <= 128)
    m0 = np.where(basev, 0.0, NEG).astype(np.float32)
    m1 = np.where(basev & (j_idx >= 64), 0.0, NEG).astype(np.float32)
    m2 = np.where(basev & (j_idx < 192), 0.0, NEG).astype(np.float32)
    c["lmask"] = np.ascontiguousarray(np.concatenate([m0, m1, m2], axis=1))
    c["idbf"] = np.eye(128, dtype=BF)
    c["idf"] = np.eye(128, dtype=np.float32)
    c["ones_f"] = np.ones((128, 128), np.float32)
    c["ones_b"] = np.ones((1, 128), BF)
    c["iota8"] = np.broadcast_to(np.arange(8, dtype=np.float32), (128, 8)).copy()
    c["im99"] = (c["iota8"] - 99.0).copy()
    sel = np.zeros((8, 8 * 128), BF)
    for e in range(8):
        sel[e, 128 * e:128 * e + 128] = 1.0
    c["sel8"] = sel
    c["fconst"] = np.array([[1e-5, 0.0, 1.0, 0.0, 0.0, 0.0, 0.0, 0.0]], np.float32)
    return c


def _host_layout(inputs):
    g = lambda k: np.ascontiguousarray(np.asarray(inputs[k], dtype=np.float32))
    L = {}
    x = np.asarray(inputs["L0_attn_x"], dtype=np.float32)
    L["_xT"] = [np.ascontiguousarray(x[b].T) for b in range(8)]
    L["wqkv"] = g("L0_attn_light_qkv_w")
    L["wlo"] = g("L0_attn_light_out_w")
    L["whq"] = g("L0_attn_heavy_q_w")
    hkv = g("L0_attn_heavy_kv_w")
    L["wk_t"] = np.ascontiguousarray(hkv[:, 0:64, :].transpose(2, 0, 1).reshape(1024, 512))
    L["wv_t"] = np.ascontiguousarray(hkv[:, 64:128, :].transpose(2, 0, 1).reshape(1024, 512))
    L["whout"] = g("L0_attn_heavy_out_w")
    nkv = g("L0_attn_heavy_null_kv")
    L["nullk"] = np.ascontiguousarray(nkv[0].reshape(4, 128))
    nvrow = np.zeros((1, 520), np.float32)
    for h in range(8):
        nvrow[0, 65 * h:65 * h + 64] = nkv[1, h]
        nvrow[0, 65 * h + 64] = 1.0
    L["nullv"] = nvrow
    L["lng"] = np.ascontiguousarray(g("L0_attn_ln_gamma").reshape(8, 128).T)
    L["lnb"] = np.ascontiguousarray(g("L0_attn_ln_beta").reshape(8, 128).T)
    L["hgam"] = np.ascontiguousarray(g("L0_attn_heavy_gamma").reshape(8, 128).T)
    L["nullq_row"] = np.ascontiguousarray(g("L0_attn_null_q_token").reshape(1, 1024))
    L["toks"] = np.ascontiguousarray(
        np.stack([g("L0_attn_q_route_tok"), g("L0_attn_kv_route_tok")], 1)
        .reshape(8, 128, 2).transpose(1, 0, 2).reshape(128, 16))
    L["wg_t"] = np.ascontiguousarray(g("L0_ffn_Wg").T)
    w1, w3, w2 = g("L0_ffn_W1"), g("L0_ffn_W3"), g("L0_ffn_W2")
    L["w1q"] = np.ascontiguousarray(
        w1.transpose(0, 2, 1).reshape(8, 1024, 4, 512).transpose(0, 2, 1, 3))
    L["w3q"] = np.ascontiguousarray(
        w3.transpose(0, 2, 1).reshape(8, 1024, 4, 512).transpose(0, 2, 1, 3))
    L["w2q"] = np.ascontiguousarray(w2.transpose(0, 2, 1).reshape(8, 4, 512, 1024))
    L["a1t"] = np.ascontiguousarray(g("L0_ffn_A1").transpose(0, 2, 1))
    L["a3t"] = np.ascontiguousarray(g("L0_ffn_A3").transpose(0, 2, 1))
    L["b1tq"] = np.ascontiguousarray(
        g("L0_ffn_B1").transpose(0, 2, 1).reshape(8, 16, 4, 512).transpose(0, 2, 1, 3))
    L["b3tq"] = np.ascontiguousarray(
        g("L0_ffn_B3").transpose(0, 2, 1).reshape(8, 16, 4, 512).transpose(0, 2, 1, 3))
    L["a2tq"] = np.ascontiguousarray(
        g("L0_ffn_A2").transpose(0, 2, 1).reshape(8, 4, 512, 16))
    L["b2t"] = np.ascontiguousarray(g("L0_ffn_B2").transpose(0, 2, 1))
    L["b1c"] = np.ascontiguousarray(g("L0_ffn_b1").reshape(8, 16, 128).transpose(0, 2, 1))
    L["b3c"] = np.ascontiguousarray(g("L0_ffn_b3").reshape(8, 16, 128).transpose(0, 2, 1))
    L["b2all"] = g("L0_ffn_b2")
    return L


def _ap_base(ap):
    t = ap.tensor
    base = getattr(t, "base_partition", 0) or 0
    try:
        off = ap.offset
        w = t.shape[1]
        return int(base + off // w) if w else int(base)
    except Exception:
        return int(base)


class Sem:
    __slots__ = ("h", "n")

    def __init__(self, nc, name):
        self.h = nc.alloc_semaphore(name)
        self.n = 0


class Prog:

    def __init__(self, nc, S):
        self.nc = nc
        self.q = {e: [] for e in ENGS}
        self.ord = {e: S["o_" + e] for e in ("vector", "scalar", "gpsimd")}

    def do(self, eng, emit, waits=(), inc=None, k=1, drain=False):
        w = [(s, n) for (s, n) in waits if n > 0]
        pend = None
        if drain:
            if inc is None:
                inc = self.ord[eng]
                k = 1
            pend = (inc, inc.n + k)
        self.q[eng].append((w, emit, None if inc is None else inc.h, k, pend))
        if getattr(self, "_stage_on", False):
            self.gseq.append(eng)
        if inc is not None:
            inc.n += k

    def dma(self, eng, dst, src, waits=(), inc=None, **kw):
        self.do(eng, lambda E: E.dma_start(dst, src, **kw), waits, inc, 16)

    def begin_stage(self, ordmap):
        self._saveq, self._saveord = self.q, self.ord
        self.q = {e: [] for e in ENGS}
        self.ord = ordmap
        self.gseq = []
        self._stage_on = True

    def end_stage(self):
        st = self.q
        self.q, self.ord = self._saveq, self._saveord
        self._stage_on = False
        return [self.gseq, st, 0]

    def feed(self, stage, frac):
        gseq, st, done = stage
        want = int(math.ceil(len(gseq) * min(1.0, frac)))
        while done < want:
            e = gseq[done]
            self.q[e].append(st[e].pop(0))
            done += 1
        stage[2] = done

    def flush(self, name):
        qs = {e: ops for e, ops in self.q.items() if ops}
        if not qs:
            return
        with self.nc.Block(name) as blk:
            for ename, ops in qs.items():
                def mk(ops):
                    def body(E):
                        seen = {}
                        prev_pend = [None]
                        for (waits, emit, inch, k, pend) in ops:
                            allw = list(waits)
                            if prev_pend[0] is not None:
                                allw.append(prev_pend[0])
                            for (s, n) in allw:
                                h = s.h
                                if seen.get(id(h), -1) < n:
                                    E.wait_ge(h, n)
                                    seen[id(h)] = n
                            ins = emit(E)
                            if inch is not None:
                                ins.then_inc(inch, k)
                            prev_pend[0] = (pend[0], pend[1]) if pend else None
                    return body
                getattr(blk, ename)(mk(ops))
        self.q = {e: [] for e in ENGS}


def build(debug_taps=()):
    nc = bass.Bass("TRN2", target_bir_lowering=False)

    dr = {}
    def din(name, shape, dt=F32):
        dr[name] = nc.dram_tensor(name, list(shape), dt, kind="ExternalInput").ap()
        return dr[name]

    xT = din("xT", (1024, 2048))
    out_d = nc.dram_tensor("out", [1024, 2048], F32, kind="ExternalOutput").ap()
    for nm, shp, dt in [
            ("wqkv", (1024, 1536), F32), ("wlo", (512, 1024), F32),
            ("whq", (1024, 512), F32), ("wk_t", (1024, 512), F32),
            ("wv_t", (1024, 512), F32), ("whout", (512, 1024), F32),
            ("nullk", (4, 128), F32), ("nullv", (1, 520), F32),
            ("lng", (128, 8), F32), ("lnb", (128, 8), F32), ("hgam", (128, 8), F32),
            ("nullq_row", (1, 1024), F32), ("toks", (128, 16), F32),
            ("wg_t", (1024, 8), F32),
            ("w1q", (8, 4, 1024, 512), F32), ("w3q", (8, 4, 1024, 512), F32),
            ("w2q", (8, 4, 512, 1024), F32),
            ("a1t", (8, 1024, 16), F32), ("a3t", (8, 1024, 16), F32),
            ("b1tq", (8, 4, 16, 512), F32), ("b3tq", (8, 4, 16, 512), F32),
            ("a2tq", (8, 4, 512, 16), F32), ("b2t", (8, 16, 1024), F32),
            ("b1c", (8, 128, 16), F32), ("b3c", (8, 128, 16), F32),
            ("b2all", (8, 1024), F32),
            ("cos2", (128, 2048), F32), ("sin2", (128, 2048), F32),
            ("lmask", (128, 768), F32), ("iota8", (128, 8), F32),
            ("im99", (128, 8), F32),
            ("idbf", (128, 128), BF16), ("idf", (128, 128), F32),
            ("ones_f", (128, 128), F32), ("ones_b", (1, 128), BF16),
            ("sel8", (8, 1024), BF16), ("fconst", (1, 8), F32)]:
        din(nm, shp, dt)

    BASE = (nc.sbuf_base + 31) & ~31
    LIMIT = BASE + 212_800
    cur = [BASE]

    def sq(name, shape, dt):
        a = cur[0]
        sz = shape[1] * (4 if dt == F32 else 2)
        cur[0] = a + ((sz + 31) & ~31)
        assert cur[0] <= LIMIT, f"SBUF overflow at {name}"
        return nc.alloc_sbuf_tensor_at(name, list(shape), dt, offset=a), a

    def sat(name, shape, dt, a):
        esz = 4 if dt == F32 else 2
        assert a + shape[1] * esz <= LIMIT, f"{name} beyond limit"
        return nc.alloc_sbuf_tensor_at(name, list(shape), dt, offset=a)

    rows1, ROWS = sq("rows1", [128, 4096], F32)
    cos2, COSA = sq("cos2", [128, 2048], F32)
    sin2, SINA = sq("sin2", [128, 2048], F32)
    lmask, LMA = sq("lmask", [128, 768], F32)
    idbf, _ = sq("idbf", [128, 128], BF16)
    idf, _ = sq("idf", [128, 128], F32)
    ones_f, OFA = sq("ones_f", [128, 128], F32)
    ones_b, _ = sq("ones_b", [1, 128], BF16)
    ones_bc, _ = sq("ones_bc", [128, 16], BF16)
    gam4, _ = sq("gam4", [128, 24], F32)
    toks, _ = sq("toks", [128, 16], F32)
    iota8, _ = sq("iota8", [128, 8], F32)
    im99, _ = sq("im99", [128, 8], F32)
    sel8, _ = sq("sel8", [8, 1024], BF16)
    nqbf, _ = sq("nqbf", [1, 1024], BF16)
    nmqbf, _ = sq("nmqbf", [1, 2048], BF16)
    smwk, _ = sq("smwk", [128, 8], F32)
    rrow, _ = sq("rrow", [1, 256], F32)
    rrbf, _ = sq("rrbf", [1, 256], BF16)
    b2allb, _ = sq("b2allb", [8, 1024], BF16)
    gatesF, _ = sq("gatesF", [8, 2048], BF16)
    wgsum, _ = sq("wgsum", [8, 8], F32)
    gtok, GTOK = sq("gtok", [128, 64], F32)
    kcol, _ = sq("kcol", [128, 32], F32)
    rankc, _ = sq("rankc", [128, 32], F32)
    mtok, _ = sq("mtok", [128, 32], F32)
    fconst, _ = sq("fconst", [1, 8], F32)
    bc0, BC0 = sq("bc0", [128, 2048], F32)
    bc1, BC1 = sq("bc1", [128, 2048], F32)
    lightT, LT = sq("lightT", [128, 8192], BF16)
    heavyT, HT = sq("heavyT", [128, 8192], BF16)
    ws, WS = sq("ws", [128, 2048], F32)
    big, BIG = sq("big", [128, 24576], F32)
    assert cur[0] <= LIMIT

    xc = [sat(f"xc{i}", [128, 2048], F32, BIG + 8192 * i) for i in range(2)]
    xsq = sat("xsq", [128, 2048], F32, BIG + 16384)
    xlc = [sat(f"xlc{i}", [128, 2048], F32, BIG + 24576 + 8192 * i) for i in range(2)]
    xlt = sat("xlt", [128, 2048], F32, BIG + 32768)
    mB = sat("mB", [128, 2048], F32, BIG + 40960)
    rB = sat("rB", [128, 2048], F32, BIG + 49152)
    c0B = sat("c0B", [128, 2048], F32, BIG + 57344)
    c1B = sat("c1B", [128, 2048], F32, BIG + 84096)
    ln2T = sat("ln2T", [128, 16384], BF16, BIG + 0)
    lq = sat("lq", [128, 8192], BF16, BIG + 32768)
    lk = sat("lk", [128, 8704], BF16, BIG + 49152)
    vsh = sat("vsh", [128, 8704], BF16, BIG + 66560)
    lsim = [sat(f"lsim{i}", [128, 256], F32, BIG + 84096 + 1024 * i) for i in range(2)]
    lattnT = [sat(f"lattnT{i}", [128, 256], BF16, BIG + 86144 + 512 * i)
              for i in range(2)]
    lattnTT = [sat(f"lattnTT{i}", [128, 256], BF16, BIG + 87168 + 512 * i)
               for i in range(2)]
    cmpw = sat("cmpw", [128, 2048], F32, BIG + 88192)
    kbkv = sat("kbkv", [128, 2048], F32, BIG + 24576)
    cscr_bf = sat("cscr_bf", [128, 2048], BF16, BIG + 0)
    xrT = sat("xrT", [128, 16384], BF16, BIG + 0)
    mB2 = sat("mB2", [128, 2048], F32, BIG + 32768)
    rsB = sat("rsB", [128, 2048], F32, BIG + 84096)
    hq = sat("hq", [128, 8192], BF16, BIG + 32768)
    hk = sat("hk", [128, 8256], BF16, BIG + 49152)
    hv = sat("hv", [128, 8840], BF16, BIG + 66560)
    attnbf = sat("attnbf", [128, 2052], BF16, BIG + 84288)
    attnT = sat("attnT", [128, 2176], BF16, BIG + 88416)
    ropea = sat("ropea", [128, 512], F32, BIG + 92800)
    ropeb = sat("ropeb", [128, 512], F32, BIG + 94848)
    rbsb = sat("rbsb", [64, 128], F32, BIG + 96896)
    denrow = sat("denrow", [1, 2048], F32, WS)
    rbig = sat("rbig", [128, 2048], F32, COSA)

    x1 = sat("x1", [128, 16384], F32, BIG + 0)
    xs2 = [sat(f"xs2_{i}", [128, 2048], F32, BIG + 65536 + 8192 * i) for i in range(2)]
    xl2T = sat("xl2T", [128, 16384], BF16, BIG + 65536)
    whofull = sat("whofull", [128, 4096], BF16, BIG + 84288)
    fsq = sat("fsq", [128, 2048], F32, COSA)
    m3B = sat("m3B", [128, 2048], F32, BC0)
    r3B = sat("r3B", [128, 2048], F32, BC1)
    mrst = [sat(f"mrst{i}", [128, 512], F32, HT + 2048 * i) for i in range(2)]
    gtmp1 = sat("gtmp1", [128, 512], BF16, SINA)
    gtmp2 = sat("gtmp2", [128, 512], BF16, SINA + 1024)
    gateb = sat("gateb", [128, 2048], BF16, SINA + 2048)
    gTq = sat("gTq", [128, 2048], BF16, LMA)

    def rowt(name, pbase, cole, parts, cols, dt=F32, arena=ROWS):
        esz = 4 if dt == F32 else 2
        t = nc.alloc_sbuf_tensor_at(name, [pbase + parts, cols], dt,
                                    offset=arena + cole * esz)
        return t[pbase:pbase + parts, :]

    m_row = rowt("m_row", 0, 0, 1, 2048)
    r_row = rowt("r_row", 0, 2048, 1, 2048)
    c0_row = rowt("c0_row", 32, 0, 1, 2048)
    c1_row = rowt("c1_row", 32, 2048, 1, 2048)
    sr_row = rowt("sr_row", 0, 0, 1, 2048, F32, HT)
    t_row = rowt("t_row", 64, 2048, 1, 2048)
    selqbf32 = sat("selqb32", [1, 2048], F32, BIG + 92800)
    s_rt = rowt("s_rt", 0, 0, 2, 2048, F32, BC0)
    rl_rt = rowt("rl_rt", 0, 0, 2, 2048, F32, BC1)
    u_rt = rowt("u_rt", 0, 0, 2, 2048, F32, HT + 8192)
    key_rt = rowt("key_rt", 0, 0, 2, 2048, F32, BIG + 8192)
    z_rt = rowt("z_rt", 0, 0, 2, 2048, F32, BIG + 88192)
    zp_rt = rowt("zp_rt", 0, 0, 2, 2048, F32, WS)
    rank_rt = rowt("rank2", 0, 0, 2, 2048, F32, BC1)
    t32 = rowt("t32", 32, 0, 1, 2048, F32, BC1)
    sel_rt = rowt("sel3", 0, 0, 2, 2048, F32, BC0)
    sc_rt = rowt("sc_rt", 0, 0, 2, 16, F32, GTOK)
    lgsb = rowt("lgsb", 0, 0, 8, 2048, F32, SINA)
    mr8 = rowt("mr8", 0, 0, 8, 2048, F32, LT)
    rr8 = rowt("rr8", 0, 0, 8, 2048, F32, LT + 8192)
    u1sb = rowt("u1sb", 0, 0, 16, 2048, BF16, ROWS)
    u3sb = rowt("u3sb", 0, 2048, 16, 2048, BF16, ROWS)
    u2sb = rowt("u2sb", 32, 0, 16, 2048, F32, ROWS)
    u2bf = rowt("u2bf", 0, 4096, 16, 2048, BF16, ROWS)
    b2te = sat("b2te", [16, 1024], BF16, WS)
    b1tqb = [sat(f"b1tqb{i}", [16, 512], BF16, WS + 2048 + 1024 * i) for i in range(2)]
    b3tqb = [sat(f"b3tqb{i}", [16, 512], BF16, WS + 4096 + 1024 * i) for i in range(2)]
    a2tqb = [sat(f"a2tqb{i}", [128, 64], BF16, WS + 6144 + 128 * i) for i in range(2)]
    a1te = sat("a1te", [128, 128], BF16, WS + 6400)
    a3te = sat("a3te", [128, 128], BF16, WS + 6656)
    b1ce = sat("b1ce", [128, 16], F32, WS + 6912)
    b3ce = sat("b3ce", [128, 16], F32, WS + 6976)
    wst = [sat(f"wst{i}", [128, 1024], BF16, WS + 2048 + 2048 * i) for i in range(2)]
    wvfull = sat("wvfull", [128, 4096], BF16, WS)
    wlofull = sat("wlofull", [128, 4096], BF16, WS)
    w1b = [sat("w1b0", [128, 4096], BF16, BC0), sat("w1b1", [128, 4096], BF16, LT)]
    w3b = [sat("w3b0", [128, 4096], BF16, BC1),
           sat("w3b1", [128, 4096], BF16, LT + 8192)]
    w2b = [sat("w2b0", [128, 4096], BF16, HT), sat("w2b1", [128, 4096], BF16, HT + 8192)]

    pSIM = nc.alloc_psum_tensor("pSIM", [128, 2048], F32)
    pT = nc.alloc_psum_tensor("pT", [128, 2048], BF16)
    pOUT = nc.alloc_psum_tensor("pOUT", [128, 128], F32)
    pMISC = nc.alloc_psum_tensor("pMISC", [128, 512], F32)
    def pch(i):
        return pSIM[:, 512 * i:512 * i + 512]

    names = ["sd0", "sd1", "sd2", "sdc", "sgc", "sgs0", "sgs1", "sgw", "sgn", "sgf",
             "sgp0", "sgp1", "mm", "mm2", "mm3", "mm4", "vec", "vec2", "vec3",
             "scl", "scl2", "pool", "sim", "mx", "ex", "tp", "tes", "tev",
             "av", "rr", "rb", "oe", "fin",
             "c1p", "c1v", "c1w", "c1s", "c1m", "c1x", "c1d"]
    names += ["o_vector", "o_scalar", "o_gpsimd",
              "c1ov", "c1os", "c1og"]
    S = {k: Sem(nc, k) for k in names}
    nc.const_aps.register(F32, 1e-5, fconst[0:1, 0:1])
    P = Prog(nc, S)
    TAPS = {}
    PEND = set()

    def tap(nm, t):
        ap = t if isinstance(t, bass.AP) else t[:, :]
        shp = [int(x) for x in ap.shape]
        tdr = nc.dram_tensor(f"tap_{nm}", shp, ap.dtype, kind="ExternalOutput").ap()
        P.dma("sync", tdr, ap,
              waits=[(s, s.n) for s in S.values()
                     if s.n > 0 and s not in PEND], inc=S["fin"])
        for _e in ("tensor", "vector", "scalar", "gpsimd"):
            P.do(_e, lambda E: E.nop(), waits=[(S["fin"], S["fin"].n)])
        TAPS[nm] = True
    xch = lambda i: xc[i % 2]
    xlch = lambda i: xlc[i % 2]

    for t, d in [(cos2[:, :], dr["cos2"]), (sin2[:, :], dr["sin2"]),
                 (idf[:, :], dr["idf"]), (ones_f[:, :], dr["ones_f"]),
                 (iota8[:, :], dr["iota8"]), (im99[:, :], dr["im99"]),
                 (gam4[:, 0:8], dr["lng"]), (gam4[:, 8:16], dr["lnb"]),
                 (gam4[:, 16:24], dr["hgam"]),
                 (toks[:, :], dr["toks"]),
                 (lmask[:, :], dr["lmask"]),
                 (idbf[:, :], dr["idbf"]), (ones_b[:, :], dr["ones_b"]),
                 (sel8[:, :], dr["sel8"]), (fconst[:, :], dr["fconst"])]:
        P.dma("sync", t, d, inc=S["sdc"])
    P.dma("gpsimd", nqbf[:, :], dr["nullq_row"], inc=S["sgc"])
    P.dma("gpsimd", b2allb[:, :], dr["b2all"], inc=S["sgc"])
    P.dma("gpsimd", ones_bc[:, :], dr["ones_f"][:, 0:16], inc=S["sgc"])
    SD_CONST = S["sdc"].n
    for _e in ("tensor", "vector", "scalar", "gpsimd"):
        P.do(_e, lambda E: E.nop(),
             waits=[(S["sdc"], SD_CONST), (S["sgc"], S["sgc"].n)])

    def row_stats(dst_m, dst_r):
        P.do("vector", lambda E: E.tensor_scalar_mul(dst_m, pSIM[0:1, 0:2048],
                                                     1.0 / 1024.0),
             waits=[(S["mm"], S["mm"].n)], inc=S["vec"], drain=True)
        P.do("vector", lambda E: E.tensor_mul(dst_r, dst_m, dst_m),
             inc=S["vec"], drain=True)
        P.do("vector", lambda E: E.scalar_tensor_tensor(
            t_row, pSIM[32:33, 0:2048], 1.0 / 1024.0, dst_r,
            ALU.mult, ALU.subtract), inc=S["vec"], drain=True)
        P.do("scalar", lambda E: E.activation(t_row, t_row, AF.Sqrt, bias=1e-5),
             waits=[(S["vec"], S["vec"].n), (S["sdc"], SD_CONST)],
             inc=S["scl"], drain=True)
        P.do("vector", lambda E: E.reciprocal(dst_r, t_row),
             waits=[(S["scl"], S["scl"].n)], inc=S["vec"], drain=True)

    def bcast(src_row, dst, wait=()):
        b = src_row.tensor.base_partition + src_row.offset // \
            (src_row.tensor.shape[1]) if False else 0
        try:
            b = int(src_row.ap[0][0] // max(1, src_row.tensor.shape[1])) if False else 0
        except Exception:
            b = 0
        b = _ap_base(src_row)
        def mmf(E):
            ins = None
            for tc in range(4):
                ins = E.matmul(pch(tc), ones_f[b:b + 1, :],
                               src_row[:, 512 * tc:512 * tc + 512],
                               start=True, stop=True)
            return ins
        P.do("tensor", mmf,
             waits=[(S["vec"], S["vec"].n), (S["scl"], S["scl"].n)] + list(wait),
             inc=S["mm"])
        P.do("scalar", lambda E: E.copy(dst[:, :], pSIM[:, 0:2048]),
             waits=[(S["mm"], S["mm"].n)], inc=S["scl"], drain=True)

    def xload(dst, dc, waits, alt=False):
        if alt and dc % 2:
            sem, eng = S["sd2"], "gpsimd"
        else:
            sem, eng = S[f"sd{dc % 2}"], "sync"
        P.dma(eng, dst, xT[128 * dc:128 * dc + 128, :], waits=waits, inc=sem)
        return [(sem, sem.n)]
    for dc in range(8):
        ld = xload(xch(dc)[:, :], dc, [(S["mm"], max(0, S["mm"].n - 1))],
                   alt=True)
        P.do("scalar", (lambda d=dc: lambda E:
                        E.activation(xsq[:, :], xch(d)[:, :], AF.Square))(),
             waits=ld + [(S["mm"], S["mm"].n)], inc=S["scl"], drain=True)
        def mk(d=dc):
            def f(E):
                ins = None
                for tc in range(4):
                    ins = E.matmul(pSIM[0:1, 512 * tc:512 * tc + 512], ones_f[:, 0:1],
                                   xch(d)[:, 512 * tc:512 * tc + 512],
                                   start=(d == 0), stop=(d == 7), skip_group_check=True)
                for tc in range(4):
                    ins = E.matmul(pSIM[32:33, 512 * tc:512 * tc + 512], ones_f[:, 1:2],
                                   xsq[:, 512 * tc:512 * tc + 512],
                                   start=(d == 0), stop=(d == 7), skip_group_check=True)
                return ins
            return f
        P.do("tensor", mk(), waits=ld + [(S["scl"], S["scl"].n)],
             inc=S["mm"])
    row_stats(m_row, r_row)
    bcast(m_row, mB)
    bcast(r_row, rB)
    P.flush("p1")

    for dc in range(8):
        ld = xload(xch(dc)[:, :], dc, [(S["vec"], max(0, S["vec"].n - 1))],
                   alt=True)
        P.do("vector", (lambda d=dc: lambda E:
                        E.tensor_sub(xlch(d)[:, :], xch(d)[:, :], mB[:, :]))(),
             waits=ld + [(S["scl"], S["scl"].n),
                         (S["mm"], max(0, S["mm"].n - 1))],
             inc=S["vec"], drain=True)
        P.do("vector", (lambda d=dc: lambda E:
                        E.tensor_mul(xlch(d)[:, :], xlch(d)[:, :], rB[:, :]))(),
             inc=S["vec"], drain=True)
        P.do("scalar", (lambda d=dc: lambda E:
                        E.activation(xsq[:, :], xlch(d)[:, :], AF.Square))(),
             waits=[(S["vec"], S["vec"].n), (S["mm"], S["mm"].n)],
             inc=S["scl"], drain=True)
        def mk(d=dc):
            def f(E):
                ins = None
                for tc in range(4):
                    ins = E.matmul(pSIM[0:1, 512 * tc:512 * tc + 512], ones_f[:, 0:1],
                                   xlch(d)[:, 512 * tc:512 * tc + 512],
                                   start=(d == 0), stop=(d == 7), skip_group_check=True)
                for tc in range(4):
                    ins = E.matmul(pSIM[64:66, 512 * tc:512 * tc + 512],
                                   toks[:, 2 * d:2 * d + 2],
                                   xlch(d)[:, 512 * tc:512 * tc + 512],
                                   start=(d == 0), stop=(d == 7), skip_group_check=True)
                for tc in range(4):
                    ins = E.matmul(pSIM[32:33, 512 * tc:512 * tc + 512], ones_f[:, 1:2],
                                   xsq[:, 512 * tc:512 * tc + 512],
                                   start=(d == 0), stop=(d == 7), skip_group_check=True)
                return ins
            return f
        P.do("tensor", mk(), waits=[(S["scl"], S["scl"].n)], inc=S["mm"])
    P.do("vector", lambda E: E.tensor_copy(s_rt, pSIM[64:66, 0:2048]),
         waits=[(S["mm"], S["mm"].n)], inc=S["vec"], drain=True)
    P.do("vector", lambda E: E.tensor_scalar_mul(c0_row, pSIM[0:1, 0:2048], 1 / 1024.0),
         inc=S["vec"], drain=True)
    P.do("vector", lambda E: E.tensor_copy(sr_row, pSIM[32:33, 0:2048]),
         inc=S["vec"], drain=True)
    P.do("vector", lambda E: E.tensor_mul(c1_row, c0_row, c0_row),
         inc=S["vec"], drain=True)
    P.do("vector", lambda E: E.scalar_tensor_tensor(
        t_row, pSIM[32:33, 0:2048], 1.0 / 1024.0, c1_row,
        ALU.mult, ALU.subtract), inc=S["vec"], drain=True)
    P.do("scalar", lambda E: E.activation(t_row, t_row, AF.Sqrt, bias=1e-5),
         waits=[(S["vec"], S["vec"].n)], inc=S["scl"], drain=True)
    P.do("scalar", lambda E: E.activation(sr_row, sr_row, AF.Sqrt),
         inc=S["scl"], drain=True)
    P.do("vector", lambda E: E.reciprocal(c1_row, t_row),
         waits=[(S["scl"], S["scl"].n)], inc=S["vec"], drain=True)
    P.do("vector", lambda E: E.tensor_scalar_max(sr_row, sr_row, 1e-12),
         inc=S["vec"], drain=True)
    P.do("vector", lambda E: E.reciprocal(sr_row, sr_row), inc=S["vec"], drain=True)
    P.do("vector", lambda E: E.tensor_scalar_mul(sr_row, sr_row, 32.0),
         inc=S["vec"], drain=True)
    P.do("vector", lambda E: E.tensor_mul(c0_row, c0_row, c1_row),
         inc=S["vec"], drain=True)
    P.do("vector", lambda E: E.tensor_scalar_mul(c0_row, c0_row, -1.0),
         inc=S["vec"], drain=True)
    bcast(c0_row, c0B)
    bcast(c1_row, c1B)
    P.flush("p2")
    if "route_s" in debug_taps:
        tap("s_rt", s_rt)

    c1ord = {"vector": S["c1ov"], "scalar": S["c1os"], "gpsimd": S["c1og"]}
    P.begin_stage(c1ord)
    P.do("gpsimd", lambda E: E.tensor_copy(rl_rt, s_rt),
         waits=[(S["vec"], S["vec"].n)], inc=S["c1p"], drain=True)
    cur_eps = 4.0
    logk = math.log(1152.0)
    for it in range(20):
        inv = 1.0 / cur_eps
        P.do("gpsimd", lambda E: E.tensor_sub(u_rt, s_rt, rl_rt),
             waits=[(S["c1s"], S["c1s"].n), (S["c1p"], S["c1p"].n)],
             inc=S["c1p"], drain=True)
        P.do("vector", lambda E: E.reduce_max(sc_rt[:, 0:1], u_rt, axis=AX.X),
             waits=[(S["c1p"], S["c1p"].n)], inc=S["c1v"], drain=True)
        P.do("vector", (lambda inv=inv: lambda E:
                        E.tensor_scalar(sc_rt[:, 1:2], sc_rt[:, 0:1], -inv, None,
                                        ALU.mult))(),
             inc=S["c1v"], drain=True)
        P.do("scalar", (lambda inv=inv: lambda E:
                        E.activation(rl_rt, u_rt, AF.Exp, bias=sc_rt[:, 1:2],
                                     scale=inv))(),
             waits=[(S["c1v"], S["c1v"].n)], inc=S["c1s"], drain=True)
        P.do("vector", lambda E: E.reduce_sum(sc_rt[:, 2:3], rl_rt, axis=AX.X),
             waits=[(S["c1s"], S["c1s"].n)], inc=S["c1v"], drain=True)
        P.do("scalar", lambda E: E.activation(sc_rt[:, 3:4], sc_rt[:, 2:3], AF.Ln),
             waits=[(S["c1v"], S["c1v"].n)], inc=S["c1s"], drain=True)
        P.do("gpsimd", lambda E: E.tensor_sub(sc_rt[:, 4:5], sc_rt[:, 3:4],
                                              sc_rt[:, 1:2]),
             waits=[(S["c1s"], S["c1s"].n)], inc=S["c1p"], drain=True)
        P.do("gpsimd", (lambda c=cur_eps: lambda E:
                        E.tensor_scalar(sc_rt[:, 5:6], sc_rt[:, 4:5], -c, c * logk,
                                        ALU.mult, ALU.add))(),
             inc=S["c1p"], drain=True)
        P.do("scalar", lambda E: E.activation(rl_rt, s_rt, AF.Relu,
                                              bias=sc_rt[:, 5:6], scale=1.0),
             waits=[(S["c1p"], S["c1p"].n)], inc=S["c1s"], drain=True)
        cur_eps = max(cur_eps * 0.7, 0.03)
    c1s1 = P.end_stage()
    PEND.update(S[k] for k in ("c1p", "c1v", "c1w", "c1s", "c1m", "c1x", "c1d",
                               "c1ov", "c1os", "c1og"))

    for dc in range(8):
        ld = xload(xs2[dc % 2][:, :], dc, [(S["vec"], max(0, S["vec"].n - 4))],
                   alt=True)
        P.do("vector", (lambda d=dc: lambda E:
                        E.tensor_sub(xlt[:, :], xs2[d % 2][:, :], mB[:, :]))(),
             waits=ld + [(S["scl"], S["scl"].n)], inc=S["vec"], drain=True)
        P.do("vector", lambda E: E.tensor_mul(xlt[:, :], xlt[:, :], rB[:, :]),
             inc=S["vec"], drain=True)
        P.do("vector", lambda E: E.tensor_mul(xlt[:, :], xlt[:, :], c1B[:, :]),
             inc=S["vec"], drain=True)
        P.do("vector", lambda E: E.tensor_add(xlt[:, :], xlt[:, :], c0B[:, :]),
             inc=S["vec"], drain=True)
        P.do("vector", (lambda d=dc: lambda E:
                        E.tensor_scalar(ln2T[:, 2048 * d:2048 * d + 2048], xlt[:, :],
                                        gam4[:, d:d + 1], gam4[:, 8 + d:8 + d + 1],
                                        ALU.mult, ALU.add))(),
             inc=S["vec"], drain=True)
    P.flush("p2b")
    if "ln2T" in debug_taps:
        tap("ln2T", ln2T)

    P.dma("gpsimd", wvfull[:, :],
          dr["wqkv"][:, 1024:1536].rearrange("(c p) f -> p c f", p=128),
          inc=S["sgw"])
    P.do("vector", lambda E: E.memset(lk[:, :], 0.0), inc=S["vec"], drain=True)
    P.do("vector", lambda E: E.memset(vsh[:, :], 0.0), inc=S["vec"], drain=True)
    vbar = S["vec"].n
    sgv = S["sgw"].n
    for tt in range(16):
        def mk(tt=tt):
            def f(E):
                ins = None
                for dc in range(8):
                    ins = E.matmul(pch(tt % 2),
                                   ln2T[:, 2048 * dc + 128 * tt:2048 * dc + 128 * tt + 128],
                                   wvfull[:, 512 * dc:512 * dc + 512],
                                   start=(dc == 0), stop=(dc == 7))
                return ins
            return f
        P.do("tensor", mk(), waits=[(S["sgw"], sgv), (S["vec"], vbar),
                                    (S["vec2"], max(0, S["vec2"].n - 1))],
             inc=S["mm"])
        P.do("vector", (lambda tt=tt: lambda E:
                        E.tensor_copy(vsh[64:128, 512 * tt:512 * tt + 512],
                                      pch(tt % 2)[0:64, :]))(),
             waits=[(S["mm"], S["mm"].n)], inc=S["vec3"], drain=True)
        P.do("vector", (lambda tt=tt: lambda E:
                        E.tensor_copy(vsh[0:64, 512 * (tt + 1):512 * (tt + 1) + 512],
                                      pch(tt % 2)[64:128, :]))(),
             inc=S["vec2"], drain=True)
    for fc in range(8):
        wsem = S[f"sgs{fc % 2}"]
        P.dma("gpsimd", wst[fc % 2][:, :],
              dr["wqkv"][:, 128 * fc:128 * fc + 128]
              .rearrange("(c p) f -> p c f", p=128),
              waits=[(S["mm2"], max(0, S["mm2"].n - 4)), (S["mm"], S["mm"].n)],
              inc=wsem)
        sgn = wsem.n
        for tc in range(4):
            k = 4 * fc + tc
            def mk(fc=fc, tc=tc, k=k):
                def f(E):
                    ins = None
                    for dc in range(8):
                        ins = E.matmul(pch(k % 2),
                                       wst[fc % 2][:, 128 * dc:128 * dc + 128],
                                       ln2T[:, 2048 * dc + 512 * tc:
                                            2048 * dc + 512 * tc + 512],
                                       start=(dc == 0), stop=(dc == 7))
                    return ins
                return f
            P.do("tensor", mk(), waits=[(wsem, sgn),
                                        (S["vec"], max(vbar, S["vec"].n - 1)),
                                        (S["vec2"], S["vec2"].n)],
                 inc=S["mm2"])
            if fc < 4:
                dst = lq[:, 2048 * fc + 512 * tc:2048 * fc + 512 * tc + 512]
            else:
                hp = fc - 4
                dst = lk[:, 2176 * hp + 64 + 512 * tc:2176 * hp + 64 + 512 * tc + 512]
            P.do("vector", (lambda d=dst, k=k: lambda E: E.tensor_copy(d, pch(k % 2)))(),
                 waits=[(S["mm2"], S["mm2"].n)], inc=S["vec"], drain=True)
    P.flush("b2")
    if "lqkv" in debug_taps:
        tap("lq", lq); tap("lk", lk); tap("vsh", vsh)

    P.begin_stage({"vector": S["c1ov"], "scalar": S["c1os"],
                   "gpsimd": S["c1og"]})

    def c1bcast(src_row, dst, wait=()):
        b = _ap_base(src_row)
        for tc in range(4):
            P.do("tensor", (lambda b=b, tc=tc: lambda E:
                            E.matmul(pMISC[:, 0:512], ones_f[b:b + 1, :],
                                     src_row[:, 512 * tc:512 * tc + 512],
                                     start=True, stop=True))(),
                 waits=[(S["c1s"], S["c1s"].n),
                        (S["c1w"], S["c1w"].n)] + list(wait),
                 inc=S["c1m"])
            P.do("scalar", (lambda dst=dst, tc=tc: lambda E:
                            E.copy(dst[:, 512 * tc:512 * tc + 512],
                                   pMISC[:, 0:512]))(),
                 waits=[(S["c1m"], S["c1m"].n)], inc=S["c1s"], drain=True)

    P.do("gpsimd", lambda E: E.tensor_scalar(key_rt, s_rt, sc_rt[:, 5:6], 0.0,
                                             ALU.add, ALU.min),
         waits=[(S["c1s"], S["c1s"].n), (S["mm"], S["mm"].n),
                (S["mm2"], S["mm2"].n)],
         inc=S["c1p"], drain=True)
    P.do("gpsimd", lambda E: E.tensor_scalar(z_rt, key_rt, 0.0, None, ALU.is_equal),
         inc=S["c1p"], drain=True)
    P.do("gpsimd", lambda E: E.memset(u_rt, 1.0), inc=S["c1p"], drain=True)
    P.do("vector", lambda E: E.tensor_tensor_scan(zp_rt, u_rt, z_rt, 0.0,
                                                  ALU.mult, ALU.add),
         waits=[(S["c1p"], S["c1p"].n)], inc=S["c1v"], drain=True)
    P.do("vector", lambda E: E.tensor_sub(zp_rt, zp_rt, z_rt), inc=S["c1v"], drain=True)
    P.do("vector", lambda E: E.tensor_mul(zp_rt, zp_rt, z_rt), inc=S["c1v"], drain=True)
    c1bcast(key_rt[0:1, :], cmpw, wait=[(S["c1v"], S["c1v"].n)])
    for c in range(16):
        P.do("tensor", (lambda c=c: lambda E:
                        E.transpose(pMISC[0:128, 0:2],
                                    key_rt[:, 128 * c:128 * c + 128], idf[0:2, 0:2]))(),
             waits=[(S["c1s"], S["c1s"].n), (S["c1w"], S["c1w"].n)],
             inc=S["c1m"])
        P.do("vector", (lambda c=c: lambda E:
                        E.tensor_copy(kcol[:, 2 * c:2 * c + 2], pMISC[0:128, 0:2]))(),
             waits=[(S["c1m"], S["c1m"].n)], inc=S["c1w"], drain=True)
    for c in range(16):
        P.do("tensor", (lambda c=c: lambda E:
                        E.transpose(pMISC[0:1, 0:128], kcol[:, 2 * c + 1:2 * c + 2],
                                    idf[:, :]))(),
             waits=[(S["c1w"], S["c1w"].n)], inc=S["c1m"])
        P.do("vector", (lambda c=c: lambda E:
                        E.tensor_copy(t_row[:, 128 * c:128 * c + 128],
                                      pMISC[0:1, 0:128]))(),
             waits=[(S["c1m"], S["c1m"].n)], inc=S["c1w"], drain=True)
    c1bcast(t_row, kbkv, wait=[(S["c1w"], S["c1w"].n)])
    KBS, KCW = S["c1s"].n, S["c1w"].n
    for tc in range(4):
        for cp in range(16):
            for r in range(2):
                kb = cmpw if r == 0 else kbkv
                slot = (cp % 2) * 2 + r
                P.do("vector", (lambda kb=kb, cp=cp, r=r, tc=tc, slot=slot:
                                lambda E:
                                E.tensor_scalar(
                                    cscr_bf[:, 512 * slot:512 * slot + 512],
                                    kb[:, 512 * tc:512 * tc + 512],
                                    kcol[:, 2 * cp + r:2 * cp + r + 1], None,
                                    ALU.is_lt))(),
                     waits=[(S["c1w"], KCW), (S["c1s"], KBS),
                            (S["c1m"], max(0, S["c1m"].n - 3))],
                     inc=S["c1v"], drain=True)
                P.do("tensor", (lambda cp=cp, r=r, slot=slot: lambda E:
                                E.matmul(pMISC[32 * r:32 * r + 1, 0:512],
                                         ones_bc[:, 0:1],
                                         cscr_bf[:, 512 * slot:512 * slot + 512],
                                         start=(cp == 0), stop=(cp == 15),
                                         skip_group_check=True))(),
                     waits=[(S["c1v"], S["c1v"].n), (S["c1w"], S["c1w"].n)],
                     inc=S["c1m"])
        P.do("vector", (lambda tc=tc: lambda E:
                        E.tensor_copy(rank_rt[0:1, 512 * tc:512 * tc + 512],
                                      pMISC[0:1, 0:512]))(),
             waits=[(S["c1m"], S["c1m"].n)], inc=S["c1w"], drain=True)
        P.do("vector", (lambda tc=tc: lambda E:
                        E.tensor_copy(t32[:, 512 * tc:512 * tc + 512],
                                      pMISC[32:33, 0:512]))(),
             inc=S["c1w"], drain=True)
    P.dma("gpsimd", rank_rt[1:2, :], t32[:, :],
          waits=[(S["c1w"], S["c1w"].n)], inc=S["c1d"])
    P.do("vector", lambda E: E.tensor_add(rank_rt, rank_rt, zp_rt),
         waits=[(S["c1w"], S["c1w"].n), (S["c1d"], S["c1d"].n)],
         inc=S["c1v"], drain=True)
    P.do("vector", lambda E: E.tensor_scalar(sel_rt, rank_rt, 1024.0, None, ALU.is_lt),
         inc=S["c1v"], drain=True)
    P.do("vector", lambda E: E.tensor_scalar(nmqbf[:, :], rank_rt[0:1, :], 1024.0,
                                             None, ALU.is_ge),
         inc=S["c1v"], drain=True)
    for c in range(16):
        P.do("tensor", (lambda c=c: lambda E:
                        E.transpose(pMISC[0:128, 0:2],
                                    sel_rt[:, 128 * c:128 * c + 128], idf[0:2, 0:2]))(),
             waits=[(S["c1v"], S["c1v"].n), (S["c1w"], S["c1w"].n)],
             inc=S["c1m"])
        P.do("vector", (lambda c=c: lambda E:
                        E.tensor_copy(mtok[:, 2 * c:2 * c + 2], pMISC[0:128, 0:2]))(),
             waits=[(S["c1m"], S["c1m"].n)], inc=S["c1w"], drain=True)
    c1stage = P.end_stage()
    PEND.update(S[k] for k in ("c1p", "c1v", "c1w", "c1s", "c1m", "c1x", "c1d",
                               "c1ov", "c1os", "c1og"))

    VC2 = S["vec"].n
    for it in range(128):
        h, wp = divmod(it, 16)
        hp, hr = divmod(h, 2)
        msel = 1 if wp == 0 else (2 if wp == 15 else 0)
        sl = it % 2
        P.do("tensor", (lambda hr=hr, hp=hp, wp=wp, sl=sl: lambda E:
                        E.matmul(pSIM[:, 1024 * sl:1024 * sl + 256],
                                 lq[64 * hr:64 * hr + 64,
                                    2048 * hp + 128 * wp:2048 * hp + 128 * wp + 128],
                                 lk[64 * hr:64 * hr + 64,
                                    2176 * hp + 128 * wp:2176 * hp + 128 * wp + 256],
                                 start=True, stop=True))(),
             waits=[(S["mx"], max(0, S["mx"].n - 1)), (S["vec"], VC2)],
             inc=S["sim"])
        P.do("vector", (lambda sl=sl, msel=msel: lambda E:
                        E.scalar_tensor_tensor(lsim[sl][:, :],
                                               pSIM[:, 1024 * sl:1024 * sl + 256],
                                               0.125,
                                               lmask[:, 256 * msel:256 * msel + 256],
                                               ALU.mult, ALU.add))(),
             waits=[(S["sim"], S["sim"].n), (S["ex"], max(0, S["ex"].n - 1))],
             inc=S["mx"], drain=True)
        P.do("vector", (lambda sl=sl: lambda E:
                        E.reduce_max(smwk[:, sl:sl + 1], lsim[sl][:, :], axis=AX.X,
                                     negate=True))(),
             inc=S["vec"], drain=True)
        P.do("scalar", (lambda sl=sl: lambda E:
                        E.activation(lattnT[sl][:, :], lsim[sl][:, :], AF.Exp,
                                     bias=smwk[:, sl:sl + 1], scale=1.0))(),
             waits=[(S["vec"], S["vec"].n), (S["tp"], max(0, S["tp"].n - 1))],
             inc=S["ex"], drain=True)
        P.do("vector", (lambda sl=sl: lambda E:
                        E.reduce_sum(smwk[:, 2 + sl:3 + sl], lattnT[sl][:, :],
                                     axis=AX.X))(),
             waits=[(S["ex"], S["ex"].n)], inc=S["vec"], drain=True)
        P.do("vector", (lambda sl=sl: lambda E:
                        E.reciprocal(smwk[:, 4 + sl:5 + sl], smwk[:, 2 + sl:3 + sl]))(),
             inc=S["vec"], drain=True)
        P.do("vector", (lambda sl=sl: lambda E:
                        E.tensor_scalar(lattnT[sl][:, :], lattnT[sl][:, :],
                                        smwk[:, 4 + sl:5 + sl], None, ALU.mult))(),
             inc=S["vec"], drain=True)
        def tp2(sl=sl):
            def f(E):
                E.transpose(pT[:, 0:128], lattnT[sl][:, 0:128], idbf[:, :])
                return E.transpose(pT[:, 128:256], lattnT[sl][:, 128:256], idbf[:, :])
            return f
        P.do("tensor", tp2(), waits=[(S["vec"], S["vec"].n), (S["tev"], S["tev"].n)],
             inc=S["tp"])
        P.do("vector", (lambda sl=sl: lambda E:
                        E.tensor_copy(lattnTT[sl][:, :], pT[:, 0:256]))(),
             waits=[(S["tp"], S["tp"].n)], inc=S["tev"], drain=True)
        def mkav(h=h, wp=wp, sl=sl):
            def f(E):
                ins = None
                for jc in range(2):
                    ins = E.matmul(pOUT[0:64, 0:128],
                                   vsh[:, 512 * (wp + jc) + 64 * h:
                                       512 * (wp + jc) + 64 * h + 64],
                                   lattnTT[sl][:, 128 * jc:128 * jc + 128],
                                   start=(jc == 0), stop=(jc == 1))
                return ins
            return f
        P.do("tensor", mkav(), waits=[(S["tev"], S["tev"].n), (S["oe"], S["oe"].n)],
             inc=S["av"])
        P.do("vector", (lambda hr=hr, hp=hp, wp=wp: lambda E:
                        E.tensor_copy(lightT[64 * hr:64 * hr + 64,
                                             2048 * hp + 128 * wp:
                                             2048 * hp + 128 * wp + 128],
                                      pOUT[0:64, 0:128]))(),
             waits=[(S["av"], S["av"].n)], inc=S["oe"], drain=True)
        P.feed(c1s1, (it + 1) / 48.0)
        P.feed(c1stage, (it - 47) / 72.0)
    P.feed(c1s1, 1.0)
    P.feed(c1stage, 1.0)
    assert not any(c1s1[1].values())
    assert not any(c1stage[1].values())
    PEND.clear()
    P.flush("c2")
    if "light" in debug_taps:
        tap("lightT", lightT)
    if "route" in debug_taps:
        tap("rank_rt", rank_rt); tap("mtok", mtok)
        tap("zp_rt", zp_rt)

    C1F = [(S[k], S[k].n) for k in ("c1p", "c1v", "c1w", "c1s", "c1m", "c1x",
                                    "c1d")]
    for _e in ("tensor", "vector", "scalar", "gpsimd"):
        P.do(_e, lambda E: E.nop(), waits=C1F)
    P.do("vector", lambda E: E.tensor_mul(t_row, r_row, sr_row),
         inc=S["vec"], drain=True)
    bcast(t_row, rsB)
    bcast(m_row, mB2)
    for dc in range(8):
        ld = xload(xs2[dc % 2][:, :], dc, [(S["vec"], max(0, S["vec"].n - 2))])
        P.do("vector", (lambda d=dc: lambda E:
                        E.tensor_sub(xs2[d % 2][:, :], xs2[d % 2][:, :], mB2[:, :]))(),
             waits=ld + [(S["scl"], S["scl"].n)], inc=S["vec"], drain=True)
        P.do("vector", (lambda d=dc: lambda E:
                        E.tensor_mul(xs2[d % 2][:, :], xs2[d % 2][:, :], rsB[:, :]))(),
             inc=S["vec"], drain=True)
        P.do("vector", (lambda d=dc: lambda E:
                        E.tensor_scalar(xrT[:, 2048 * d:2048 * d + 2048],
                                        xs2[d % 2][:, :],
                                        gam4[:, 16 + d:16 + d + 1], None, ALU.mult))(),
             inc=S["vec"], drain=True)
    P.flush("d0a")
    if "xrT" in debug_taps:
        tap("xrT", xrT)

    def rope_gen(wdr, dst, is_q, stride):
        for fc in range(4):
            wsem = S[f"sgs{fc % 2}"]
            P.dma("gpsimd", wst[fc % 2][:, :],
                  wdr[:, 128 * fc:128 * fc + 128]
                  .rearrange("(c p) f -> p c f", p=128),
                  waits=[(S["mm2"], max(0, S["mm2"].n - 4))], inc=wsem)
            sgn = wsem.n
            for tc in range(4):
                k = 4 * fc + tc
                def mk(fc=fc, tc=tc, k=k):
                    def f(E):
                        ins = None
                        for dc in range(8):
                            ins = E.matmul(pch(k % 2),
                                           wst[fc % 2][:, 128 * dc:128 * dc + 128],
                                           xrT[:, 2048 * dc + 512 * tc:
                                               2048 * dc + 512 * tc + 512],
                                           start=(dc == 0), stop=(dc == 7))
                        return ins
                    return f
                P.do("tensor", mk(),
                     waits=[(wsem, sgn), (S["vec"], S["vec"].n),
                            (S["vec2"], max(0, S["vec2"].n - 1))], inc=S["mm2"])
                sc0 = 0.125 if is_q else 1.0
                P.do("vector", (lambda k=k, tc=tc, sc0=sc0: lambda E:
                                E.scalar_tensor_tensor(
                                    ropea[:, :], pch(k % 2), sc0,
                                    cos2[:, 512 * tc:512 * tc + 512],
                                    ALU.mult, ALU.mult))(),
                     waits=[(S["mm2"], S["mm2"].n)], inc=S["vec3"], drain=True)
                for bb in range(4):
                    P.do("vector", (lambda k=k, tc=tc, bb=bb, sc0=sc0: lambda E:
                                    E.scalar_tensor_tensor(
                                        ropeb[32 * bb:32 * bb + 32, :],
                                        pch(k % 2)[32 * (bb ^ 1):32 * (bb ^ 1) + 32, :],
                                        sc0,
                                        sin2[32 * bb:32 * bb + 32,
                                             512 * tc:512 * tc + 512],
                                        ALU.mult, ALU.mult))(),
                         inc=S["vec3"], drain=True)
                dsl = dst[:, stride * fc + 512 * tc:stride * fc + 512 * tc + 512]
                P.do("vector", (lambda d=dsl: lambda E:
                                E.tensor_add(d, ropea[:, :], ropeb[:, :]))(),
                     inc=S["vec2"], drain=True)

    rope_gen(dr["whq"], hq, True, 2048)
    rope_gen(dr["wk_t"], hk, False, 2064)
    P.flush("d0qk")

    P.dma("gpsimd", wvfull[:, :], dr["wv_t"].rearrange("(c p) f -> p c f", p=128),
          waits=[(S["mm2"], S["mm2"].n)], inc=S["sgw"])
    for hp2 in range(4):
        P.dma("gpsimd", hk[:, 2064 * hp2 + 2048:2064 * hp2 + 2049],
              dr["nullk"][hp2, :].rearrange("(p o) -> p o", o=1),
              waits=[(S["vec2"], S["vec2"].n)], inc=S["sgn"])
    P.dma("gpsimd", hv[0:1, 520 * 16:520 * 16 + 520], dr["nullv"], inc=S["sgn"])
    sgv = S["sgw"].n
    sgnv = S["sgn"].n
    for tt in range(16):
        def mk(tt=tt):
            def f(E):
                ins = None
                for dc in range(8):
                    ins = E.matmul(pch(tt % 2),
                                   xrT[:, 2048 * dc + 128 * tt:2048 * dc + 128 * tt + 128],
                                   wvfull[:, 512 * dc:512 * dc + 512],
                                   start=(dc == 0), stop=(dc == 7))
                return ins
            return f
        P.do("tensor", mk(), waits=[(S["sgw"], sgv), (S["vec3"], S["vec3"].n),
                                    (S["vec"], max(0, S["vec"].n - 1))],
             inc=S["mm"])
        def mkev(tt=tt):
            def f(E):
                ins = None
                for h in range(8):
                    ins = E.tensor_scalar(
                        hv[:, 520 * tt + 65 * h:520 * tt + 65 * h + 64],
                        pch(tt % 2)[:, 64 * h:64 * h + 64],
                        mtok[:, 2 * tt + 1:2 * tt + 2], None, ALU.mult)
                return ins
            return f
        P.do("vector", mkev(), waits=[(S["mm"], S["mm"].n)], drain=True, inc=None)
        def mkcp(tt=tt):
            def f(E):
                ins = None
                for h in range(8):
                    ins = E.tensor_copy(
                        hv[:, 520 * tt + 65 * h + 64:520 * tt + 65 * h + 65],
                        mtok[:, 2 * tt + 1:2 * tt + 2])
                return ins
            return f
        P.do("vector", mkcp(), inc=S["vec"], drain=True)
    P.flush("d0v")
    if "hqkv" in debug_taps:
        tap("hq", hq)
        for _hp in range(4):
            tap(f"hk{_hp}", hk[:, 2064 * _hp:2064 * _hp + 2049])
        tap("hv", hv[:, 0:8320]); tap("hvn", hv[0:1, 8320:8840])

    P.do("vector", lambda E: E.tensor_scalar(selqbf32[:, :], rank_rt[0:1, :], 1024.0,
                                             None, ALU.is_lt),
         waits=[(S["vec2"], S["vec2"].n)], inc=S["vec"], drain=True)
    B = {k: S[k].n for k in ("sim", "mx", "ex", "tp", "tes", "tev", "av",
                             "rr", "rb", "oe")}
    VD0 = S["vec"].n
    SGN0 = S["sgn"].n
    tailT, tailV, tailH = [], [], [None]
    for it in range(128):
        h, qc = divmod(it, 16)
        hp, hr = divmod(h, 2)
        qsl = hq[64 * hr:64 * hr + 64, 2048 * hp + 128 * qc:2048 * hp + 128 * qc + 128]

        for kc in range(4):
            P.do("tensor", (lambda kc=kc, qsl=qsl, hp=hp, hr=hr: lambda E:
                            E.matmul(pSIM[:, 512 * kc:512 * kc + 512], qsl,
                                     hk[64 * hr:64 * hr + 64,
                                        2064 * hp + 512 * kc:2064 * hp + 512 * kc + 512],
                                     start=True, stop=True))(),
                 waits=[(S["ex"], B["ex"] + max(0, 3 * it - (2 if kc < 2 else 1))),
                        (S["vec"], VD0), (S["sgn"], SGN0)],
                 inc=S["sim"])
        P.do("tensor", (lambda qsl=qsl, hp=hp, hr=hr: lambda E:
                        E.matmul(pMISC[:, 384:385], qsl,
                                 hk[64 * hr:64 * hr + 64,
                                    2064 * hp + 2048:2064 * hp + 2049],
                                 start=True, stop=True))(),
             waits=[(S["ex"], B["ex"] + max(0, 3 * it)), (S["rb"], S["rb"].n)],
             inc=S["sim"])
        for th in tailT:
            th()
        tailT = []
        for kc in range(4):
            P.do("vector", (lambda kc=kc: lambda E:
                            E.reduce_max(smwk[:, 2 + kc:3 + kc],
                                         pSIM[:, 512 * kc:512 * kc + 512],
                                         axis=AX.X))(),
                 waits=[(S["sim"], B["sim"] + 5 * it + kc + 1),
                        (S["ex"], B["ex"] + max(0, 3 * (it - 1)))],
                 drain=True)
        P.do("vector", lambda E: E.tensor_copy(smwk[:, 6:7], pMISC[:, 384:385]),
             waits=[(S["sim"], B["sim"] + 5 * it + 5)], drain=True)
        P.do("vector", (lambda sl=it % 2: lambda E:
                        E.reduce_max(smwk[:, sl:sl + 1], smwk[:, 2:7], axis=AX.X,
                                     negate=True))(),
             inc=S["mx"], drain=True)
        for th in tailV:
            th()
        tailV = []
        if tailH[0] is not None:
            tailH[0]()
            tailH[0] = None
        oeW = S["oe"].n
        P.do("scalar", (lambda sl=it % 2: lambda E:
                        E.activation(attnbf[:, 0:1024], pSIM[:, 0:1024], AF.Exp,
                                     bias=smwk[:, sl:sl + 1], scale=1.0))(),
             waits=[(S["mx"], B["mx"] + it + 1), (S["tp"], B["tp"] + 3 * it)],
             inc=S["ex"], drain=True)
        P.do("scalar", (lambda sl=it % 2: lambda E:
                        E.activation(attnbf[:, 1024:2048], pSIM[:, 1024:2048], AF.Exp,
                                     bias=smwk[:, sl:sl + 1], scale=1.0))(),
             inc=S["ex"], drain=True)
        P.do("scalar", (lambda sl=it % 2: lambda E:
                        E.activation(attnbf[:, 2048:2049], pMISC[:, 384:385], AF.Exp,
                                     bias=smwk[:, sl:sl + 1], scale=1.0))(),
             inc=S["ex"], drain=True)
        P.do("scalar", lambda E: E.copy(attnT[:, 0:1024], pT[:, 0:1024]),
             waits=[(S["tp"], B["tp"] + 3 * it + 1)], inc=S["tes"], drain=True)
        P.do("scalar", lambda E: E.copy(attnT[:, 1024:2048], pT[:, 1024:2048]),
             waits=[(S["tp"], B["tp"] + 3 * it + 2)], inc=S["tes"], drain=True)
        def tpb(j0, half):
            def f(E):
                ins = None
                for j in range(j0, j0 + 8):
                    ins = E.transpose(pT[:, 1024 * half + 128 * (j - j0):
                                         1024 * half + 128 * (j - j0) + 128],
                                      attnbf[:, 128 * j:128 * j + 128], idbf[:, :])
                return ins
            return f
        def av(j0, j1, h=h, start=False):
            def f(E):
                ins = None
                for j in range(j0, j1):
                    ins = E.matmul(pOUT[0:65, 0:128],
                                   hv[:, 520 * j + 65 * h:520 * j + 65 * h + 65],
                                   attnT[:, 128 * j:128 * j + 128],
                                   start=(start and j == j0), stop=False)
                return ins
            return f
        tailT = [
            (lambda it=it: P.do("tensor", tpb(0, 0),
                 waits=[(S["ex"], B["ex"] + 3 * it + 1),
                        (S["tes"], B["tes"] + max(0, 2 * it - 1)),
                        (S["tev"], B["tev"] + it)], inc=S["tp"])),
            (lambda it=it: P.do("tensor", tpb(8, 1),
                 waits=[(S["ex"], B["ex"] + 3 * it + 2),
                        (S["tes"], B["tes"] + 2 * it)], inc=S["tp"])),
            (lambda it=it: P.do("tensor", (lambda: lambda E:
                 E.transpose(pT[0:1, 0:128], attnbf[:, 2048:2049], idbf[:, :]))(),
                 waits=[(S["ex"], B["ex"] + 3 * it + 3),
                        (S["tes"], B["tes"] + 2 * it + 1)], inc=S["tp"])),
            (lambda it=it, oeW=oeW: P.do("tensor", av(0, 8, start=True),
                 waits=[(S["oe"], oeW), (S["tes"], B["tes"] + 2 * it + 1)])),
            (lambda it=it: P.do("tensor", av(8, 16),
                 waits=[(S["tes"], B["tes"] + 2 * it + 2)])),
            (lambda it=it, h=h: P.do("tensor", (lambda h=h: lambda E:
                 E.matmul(pOUT[0:65, 0:128],
                          hv[0:1, 520 * 16 + 65 * h:520 * 16 + 65 * h + 65],
                          attnT[0:1, 2048:2176], start=False, stop=True))(),
                 waits=[(S["tev"], B["tev"] + it + 1)], inc=S["av"])),
        ]
        tailV = [
            (lambda it=it: P.do("vector", lambda E:
                 E.tensor_copy(attnT[0:1, 2048:2176], pT[0:1, 0:128]),
                 waits=[(S["tp"], B["tp"] + 3 * it + 3)], inc=S["tev"], drain=True)),
            (lambda it=it, qc=qc: P.do("vector", (lambda qc=qc: lambda E:
                 E.tensor_copy(denrow[0:1, 128 * qc:128 * qc + 128],
                               pOUT[64:65, 0:128]))(),
                 waits=[(S["av"], B["av"] + it + 1)], drain=True)),
            (lambda it=it, hr=hr, hp=hp, qc=qc: P.do("vector",
                 (lambda hr=hr, hp=hp, qc=qc: lambda E:
                  E.tensor_copy(heavyT[64 * hr:64 * hr + 64,
                                       2048 * hp + 128 * qc:
                                       2048 * hp + 128 * qc + 128],
                                pOUT[0:64, 0:128]))(),
                 inc=S["oe"], drain=True)),
        ]
        if qc == 15:
            def headend(h=h, hr=hr, hp=hp, it=it):
                P.do("vector", lambda E: E.reciprocal(denrow[0:1, :],
                                                      denrow[0:1, :]), drain=True)
                P.do("vector", (lambda: lambda E:
                                E.tensor_tensor(denrow[0:1, :], denrow[0:1, :],
                                                selqbf32[0:1, :], ALU.mult))(),
                     inc=S["rr"], drain=True)
                for tcq in range(8):
                    P.do("tensor", (lambda tcq=tcq: lambda E:
                                    E.matmul(pMISC[0:128, 0:256], ones_f[0:1, :],
                                             denrow[0:1, 256 * tcq:256 * tcq + 256],
                                             start=True, stop=True))(),
                         waits=[(S["rr"], S["rr"].n), (S["rb"], S["rb"].n),
                                (S["ex"], min(B["ex"] + 3 * (it + 1) + 3,
                                              B["ex"] + 3 * 128)),
                                (S["mx"], min(B["mx"] + it + 2,
                                              B["mx"] + 128))],
                         inc=S["mm3"])
                    P.do("vector", (lambda tcq=tcq: lambda E:
                                    E.tensor_copy(rbig[:, 256 * tcq:256 * tcq + 256],
                                                  pMISC[0:128, 0:256]))(),
                         waits=[(S["mm3"], S["mm3"].n)], inc=S["rb"], drain=True)
                P.do("vector", (lambda hr=hr, hp=hp: lambda E:
                                E.tensor_mul(heavyT[64 * hr:64 * hr + 64,
                                                    2048 * hp:2048 * hp + 2048],
                                             heavyT[64 * hr:64 * hr + 64,
                                                    2048 * hp:2048 * hp + 2048],
                                             rbig[64 * hr:64 * hr + 64, :]))(),
                     inc=S["oe"], drain=True)
            tailH[0] = headend
    for th in tailT:
        th()
    for th in tailV:
        th()
    if tailH[0] is not None:
        tailH[0]()
    P.flush("d")
    if "heavy" in debug_taps:
        tap("heavyT", heavyT)

    P.dma("gpsimd", wlofull[:, :], dr["wlo"].rearrange("(c p) f -> p c f", p=128),
          waits=[(S["mm"], S["mm"].n), (S["mm2"], S["mm2"].n),
                 (S["sgw"], S["sgw"].n)], inc=S["sgw"])
    P.dma("gpsimd", whofull[:, :], dr["whout"].rearrange("(c p) f -> p c f", p=128),
          waits=[(S["ex"], S["ex"].n)], inc=S["sgw"])
    sgE = S["sgw"].n
    for dc in range(8):
        ld = xload(xs2[dc % 2][:, :], dc, [(S["vec"], max(0, S["vec"].n - 4))])
        for tc in range(4):
            k = 4 * dc + tc
            def mk(dc=dc, tc=tc, k=k):
                def f(E):
                    p = pch(k % 2)
                    for fc in range(4):
                        E.matmul(p, wlofull[:, 1024 * fc + 128 * dc:
                                            1024 * fc + 128 * dc + 128],
                                 lightT[:, 2048 * fc + 512 * tc:
                                        2048 * fc + 512 * tc + 512],
                                 start=(fc == 0), stop=False)
                    for fc in range(4):
                        E.matmul(p, whofull[:, 1024 * fc + 128 * dc:
                                            1024 * fc + 128 * dc + 128],
                                 heavyT[:, 2048 * fc + 512 * tc:
                                        2048 * fc + 512 * tc + 512],
                                 start=False, stop=False)
                    return E.matmul(p, nqbf[:, 128 * dc:128 * dc + 128],
                                    nmqbf[:, 512 * tc:512 * tc + 512],
                                    start=False, stop=True)
                return f
            P.do("tensor", mk(),
                 waits=[(S["sgw"], sgE)] + ld + [(S["oe"], S["oe"].n),
                        (S["vec"], max(0, S["vec"].n - 1))],
                 inc=S["mm"])
            P.do("vector", (lambda dc=dc, tc=tc, k=k: lambda E:
                            E.scalar_tensor_tensor(
                                x1[:, 2048 * dc + 512 * tc:2048 * dc + 512 * tc + 512],
                                pch(k % 2), 1.0,
                                xs2[dc % 2][:, 512 * tc:512 * tc + 512],
                                ALU.mult, ALU.add))(),
                 waits=[(S["mm"], S["mm"].n)], inc=S["vec"], drain=True)
    P.flush("e")
    if "x1" in debug_taps:
        tap("x1", x1)

    for dc in range(8):
        P.do("scalar", (lambda d=dc: lambda E:
                        E.activation(fsq[:, :], x1[:, 2048 * d:2048 * d + 2048],
                                     AF.Square))(),
             waits=[(S["mm"], S["mm"].n), (S["vec"], S["vec"].n)],
             inc=S["scl"], drain=True)
        def mk(d=dc):
            def f(E):
                ins = None
                for tc in range(4):
                    ins = E.matmul(pSIM[0:1, 512 * tc:512 * tc + 512], ones_f[:, 0:1],
                                   x1[:, 2048 * d + 512 * tc:2048 * d + 512 * tc + 512],
                                   start=(d == 0), stop=(d == 7), skip_group_check=True)
                for tc in range(4):
                    ins = E.matmul(pSIM[32:33, 512 * tc:512 * tc + 512], ones_f[:, 1:2],
                                   fsq[:, 512 * tc:512 * tc + 512],
                                   start=(d == 0), stop=(d == 7), skip_group_check=True)
                return ins
            return f
        P.do("tensor", mk(), waits=[(S["scl"], S["scl"].n)], inc=S["mm"])
    row_stats(m_row, r_row)
    bcast(m_row, m3B)
    bcast(r_row, r3B)
    for dc in range(8):
        P.dma("gpsimd", fsq[:, 8 * dc:8 * dc + 8],
              dr["wg_t"][128 * dc:128 * dc + 128, :],
              waits=[(S["mm"], S["mm"].n)], inc=S["sgf"])
    def gsum(E):
        ins = None
        for dc in range(8):
            ins = E.matmul(pMISC[0:8, 0:1], fsq[:, 8 * dc:8 * dc + 8],
                           ones_f[:, 0:1], start=(dc == 0), stop=(dc == 7))
        return ins
    P.do("tensor", gsum, waits=[(S["sgf"], S["sgf"].n), (S["scl"], S["scl"].n)],
         inc=S["mm"])
    def glog(E):
        ins = None
        for dc in range(8):
            for tc in range(4):
                ins = E.matmul(pSIM[64:72, 512 * tc:512 * tc + 512],
                               fsq[:, 8 * dc:8 * dc + 8],
                               x1[:, 2048 * dc + 512 * tc:2048 * dc + 512 * tc + 512],
                               start=(dc == 0), stop=(dc == 7), skip_group_check=True)
        return ins
    P.do("tensor", glog, inc=S["mm"])
    P.do("vector", lambda E: E.tensor_copy(wgsum[:, 0:1], pMISC[0:8, 0:1]),
         waits=[(S["mm"], S["mm"].n)], inc=S["vec"], drain=True)
    P.do("vector", lambda E: E.tensor_copy(lgsb, pSIM[64:72, 0:2048]),
         inc=S["vec"], drain=True)
    P.do("vector", lambda E: E.tensor_copy(mr8, m3B[0:8, :]), inc=S["vec"], drain=True)
    P.do("vector", lambda E: E.tensor_copy(rr8, r3B[0:8, :]), inc=S["vec"], drain=True)
    P.do("vector", lambda E: E.tensor_scalar(mr8, mr8, wgsum[:, 0:1], None, ALU.mult),
         inc=S["vec"], drain=True)
    P.do("vector", lambda E: E.tensor_sub(lgsb, lgsb, mr8), inc=S["vec"], drain=True)
    P.do("vector", lambda E: E.tensor_mul(lgsb, lgsb, rr8), inc=S["vec"], drain=True)
    for c in range(16):
        P.do("tensor", (lambda c=c: lambda E:
                        E.transpose(pMISC[0:128, 0:8], lgsb[:, 128 * c:128 * c + 128],
                                    idf[0:8, 0:8]))(),
             waits=[(S["vec"], S["vec"].n), (S["vec2"], S["vec2"].n)], inc=S["mm"])
        P.do("vector", lambda E: E.tensor_copy(gtok[:, 0:8], pMISC[0:128, 0:8]),
             waits=[(S["mm"], S["mm"].n)], inc=S["vec"], drain=True)
        P.do("vector", lambda E: E.reduce_max(gtok[:, 8:9], gtok[:, 0:8], axis=AX.X,
                                              negate=True),
             inc=S["vec"], drain=True)
        P.do("scalar", lambda E: E.activation(gtok[:, 24:32], gtok[:, 0:8], AF.Exp,
                                              bias=gtok[:, 8:9], scale=1.0),
             waits=[(S["vec"], S["vec"].n)], inc=S["scl"], drain=True)
        P.do("vector", lambda E: E.reduce_sum(gtok[:, 9:10], gtok[:, 24:32], axis=AX.X),
             waits=[(S["scl"], S["scl"].n)], inc=S["vec"], drain=True)
        P.do("vector", lambda E: E.reciprocal(gtok[:, 10:11], gtok[:, 9:10]),
             inc=S["vec"], drain=True)
        P.do("vector", lambda E: E.tensor_scalar(gtok[:, 24:32], gtok[:, 24:32],
                                                 gtok[:, 10:11], None, ALU.mult),
             inc=S["vec"], drain=True)
        P.do("vector", lambda E: E.reduce_max(gtok[:, 11:12], gtok[:, 24:32], axis=AX.X),
             inc=S["vec"], drain=True)
        P.do("vector", lambda E: E.tensor_scalar(gtok[:, 32:40], gtok[:, 24:32],
                                                 gtok[:, 11:12], None, ALU.is_equal),
             inc=S["vec"], drain=True)
        P.do("vector", lambda E: E.scalar_tensor_tensor(gtok[:, 40:48], gtok[:, 32:40],
                                                        1.0, im99[:, :],
                                                        ALU.mult, ALU.mult),
             inc=S["vec"], drain=True)
        P.do("vector", lambda E: E.tensor_scalar(gtok[:, 40:48], gtok[:, 40:48], 99.0,
                                                 None, ALU.add),
             inc=S["vec"], drain=True)
        P.do("vector", lambda E: E.tensor_reduce(gtok[:, 12:13], gtok[:, 40:48],
                                                 axis=AX.X, op=ALU.min),
             inc=S["vec"], drain=True)
        P.do("vector", lambda E: E.tensor_scalar(gtok[:, 32:40], iota8[:, :],
                                                 gtok[:, 12:13], None, ALU.is_equal),
             inc=S["vec"], drain=True)
        P.do("vector", lambda E: E.scalar_tensor_tensor(gtok[:, 48:56], gtok[:, 32:40],
                                                        -2.0, gtok[:, 24:32],
                                                        ALU.mult, ALU.add),
             inc=S["vec"], drain=True)
        P.do("vector", lambda E: E.reduce_max(gtok[:, 13:14], gtok[:, 48:56], axis=AX.X),
             inc=S["vec"], drain=True)
        P.do("vector", lambda E: E.tensor_scalar(gtok[:, 48:56], gtok[:, 48:56],
                                                 gtok[:, 13:14], None, ALU.is_equal),
             inc=S["vec"], drain=True)
        P.do("vector", lambda E: E.scalar_tensor_tensor(gtok[:, 48:56], gtok[:, 48:56],
                                                        1.0, im99[:, :],
                                                        ALU.mult, ALU.mult),
             inc=S["vec"], drain=True)
        P.do("vector", lambda E: E.tensor_scalar(gtok[:, 48:56], gtok[:, 48:56], 99.0,
                                                 None, ALU.add),
             inc=S["vec"], drain=True)
        P.do("vector", lambda E: E.tensor_reduce(gtok[:, 14:15], gtok[:, 48:56],
                                                 axis=AX.X, op=ALU.min),
             inc=S["vec"], drain=True)
        P.do("vector", lambda E: E.tensor_scalar(gtok[:, 40:48], iota8[:, :],
                                                 gtok[:, 14:15], None, ALU.is_equal),
             inc=S["vec"], drain=True)
        P.do("vector", lambda E: E.tensor_add(gtok[:, 15:16], gtok[:, 11:12],
                                              gtok[:, 13:14]),
             inc=S["vec"], drain=True)
        P.do("vector", lambda E: E.reciprocal(gtok[:, 15:16], gtok[:, 15:16]),
             inc=S["vec"], drain=True)
        P.do("vector", lambda E: E.tensor_scalar(gtok[:, 32:40], gtok[:, 32:40],
                                                 gtok[:, 11:12], gtok[:, 15:16],
                                                 ALU.mult, ALU.mult),
             inc=S["vec"], drain=True)
        P.do("vector", lambda E: E.tensor_scalar(gtok[:, 40:48], gtok[:, 40:48],
                                                 gtok[:, 13:14], gtok[:, 15:16],
                                                 ALU.mult, ALU.mult),
             inc=S["vec"], drain=True)
        P.do("vector", lambda E: E.tensor_add(gtok[:, 0:8], gtok[:, 32:40],
                                              gtok[:, 40:48]),
             inc=S["vec"], drain=True)
        P.do("tensor", (lambda c=c: lambda E:
                        E.transpose(pMISC[0:8, 0:128], gtok[:, 0:8], idf[:, :]))(),
             waits=[(S["vec"], S["vec"].n)], inc=S["mm2"])
        P.do("vector", (lambda c=c: lambda E:
                        E.tensor_copy(gatesF[:, 128 * c:128 * c + 128],
                                      pMISC[0:8, 0:128]))(),
             waits=[(S["mm2"], S["mm2"].n)], inc=S["vec2"], drain=True)
    for dc in range(8):
        for tc in range(4):
            k = 4 * dc + tc
            st = mrst[k % 2]
            P.do("vector", (lambda dc=dc, tc=tc, st=st: lambda E:
                            E.scalar_tensor_tensor(
                                st[:, :],
                                x1[:, 2048 * dc + 512 * tc:2048 * dc + 512 * tc + 512],
                                1.0, m3B[:, 512 * tc:512 * tc + 512],
                                ALU.mult, ALU.subtract))(),
                 waits=[(S["scl2"], max(0, S["scl2"].n - 1))], inc=S["vec"], drain=True)
            P.do("vector", (lambda tc=tc, st=st: lambda E:
                            E.tensor_mul(st[:, :], st[:, :],
                                         r3B[:, 512 * tc:512 * tc + 512]))(),
                 inc=S["vec"], drain=True)
            P.do("scalar", (lambda dc=dc, tc=tc, st=st: lambda E:
                            E.copy(xl2T[:, 2048 * dc + 512 * tc:
                                        2048 * dc + 512 * tc + 512], st[:, :]))(),
                 waits=[(S["vec"], S["vec"].n)], inc=S["scl2"], drain=True)
    P.flush("f")
    if "gates" in debug_taps:
        tap("gatesF", gatesF)
    if "xl2T" in debug_taps:
        tap("xl2T", xl2T)

    EQ = [(e, q) for e in range(8) for q in range(4)]
    sg_at = {}

    def wdma(eq_i, extra=()):
        e, q = EQ[eq_i]
        par = eq_i % 2
        sem = S[f"sgp{par}"]
        w = list(extra) + [(S["mm4"], max(0, eq_i - 1))]
        P.dma("gpsimd", w1b[par][:, :],
              dr["w1q"][e, q].rearrange("(c p) f -> p c f", p=128),
              waits=w, inc=sem)
        P.dma("gpsimd", w3b[par][:, :],
              dr["w3q"][e, q].rearrange("(c p) f -> p c f", p=128), inc=sem)
        P.dma("gpsimd", w2b[par][:, :],
              dr["w2q"][e, q].rearrange("(c p) f -> p c f", p=128), inc=sem)
        P.dma("gpsimd", b1tqb[par][:, :], dr["b1tq"][e, q], inc=sem)
        P.dma("gpsimd", b3tqb[par][:, :], dr["b3tq"][e, q], inc=sem)
        P.dma("gpsimd", a2tqb[par][:, :],
              dr["a2tq"][e, q].rearrange("(c p) f -> p c f", p=128), inc=sem)
        if q == 0:
            P.dma("gpsimd", a1te[:, :],
                  dr["a1t"][e].rearrange("(c p) f -> p c f", p=128),
                  waits=[(S["mm4"], eq_i)], inc=sem)
            P.dma("gpsimd", a3te[:, :],
                  dr["a3t"][e].rearrange("(c p) f -> p c f", p=128), inc=sem)
            P.dma("gpsimd", b2te[:, :], dr["b2t"][e], inc=sem)
            P.dma("gpsimd", b1ce[:, :], dr["b1c"][e], inc=sem)
            P.dma("gpsimd", b3ce[:, :], dr["b3c"][e], inc=sem)
        sg_at[eq_i] = (sem, sem.n)

    wdma(0, extra=[(S["scl2"], S["scl2"].n), (S["vec2"], S["vec2"].n),
                   (S["vec"], S["vec"].n)])
    for eq_i in range(32):
        e, q = EQ[eq_i]
        par = eq_i % 2
        if eq_i + 1 < 32:
            wdma(eq_i + 1)
        if q == 0:
            P.do("vector", lambda E: E.memset(u2sb, 0.0),
                 waits=[(S["mm2"], S["mm2"].n)], inc=S["vec3"], drain=True)
        for tc in range(4):
            if q == 0:
                def mku(att, tc=tc):
                    def f(E):
                        ins = None
                        for dc in range(8):
                            ins = E.matmul(pMISC[0:16, 0:512],
                                           att[:, 16 * dc:16 * dc + 16],
                                           xl2T[:, 2048 * dc + 512 * tc:
                                                2048 * dc + 512 * tc + 512],
                                           start=(dc == 0), stop=(dc == 7))
                        return ins
                    return f
                P.do("tensor", mku(a1te),
                     waits=[sg_at[eq_i], (S["scl2"], S["scl2"].n),
                            (S["vec3"], S["vec3"].n), (S["vec"], S["vec"].n),
                            (S["sim"], S["sim"].n)],
                     inc=S["mm"])
                P.do("vector", (lambda tc=tc: lambda E:
                                E.tensor_copy(u1sb[:, 512 * tc:512 * tc + 512],
                                              pMISC[0:16, 0:512]))(),
                     waits=[(S["mm"], S["mm"].n)], inc=S["vec"], drain=True)
                P.do("tensor", mku(a3te), waits=[(S["vec"], S["vec"].n)], inc=S["mm"])
                P.do("vector", (lambda tc=tc: lambda E:
                                E.tensor_copy(u3sb[:, 512 * tc:512 * tc + 512],
                                              pMISC[0:16, 0:512]))(),
                     waits=[(S["mm"], S["mm"].n)], inc=S["vec"], drain=True)
            if q == 0:
                P.do("tensor", (lambda e=e, tc=tc: lambda E:
                                E.matmul(pch(3), sel8[:, 128 * e:128 * e + 128],
                                         gatesF[:, 512 * tc:512 * tc + 512],
                                         start=True, stop=True))(),
                     waits=[(S["tev"], S["tev"].n), (S["vec2"], S["vec2"].n),
                            (S["oe"], S["oe"].n), (S["scl2"], S["scl2"].n)],
                     inc=S["rb"])
                P.do("vector", (lambda tc=tc: lambda E:
                                E.tensor_copy(gateb[:, 512 * tc:512 * tc + 512],
                                              pch(3)))(),
                     waits=[(S["rb"], S["rb"].n), (S["vec"], S["vec"].n)],
                     inc=S["tev"], drain=True)
            UTC = S["vec"].n
            for hc in range(4):
                hcg = 4 * q + hc
                b1bank = 2 * (hc % 2)
                b3bank = 1 + 2 * (hc % 2)
                def h13(wb, btq, usb, bank, tc=tc, hc=hc):
                    def f(E):
                        for dc in range(8):
                            E.matmul(pSIM[:, 512 * bank:512 * bank + 512],
                                     wb[:, 512 * dc + 128 * hc:512 * dc + 128 * hc + 128],
                                     xl2T[:, 2048 * dc + 512 * tc:
                                          2048 * dc + 512 * tc + 512],
                                     start=(dc == 0), stop=False)
                        return E.matmul(pSIM[:, 512 * bank:512 * bank + 512],
                                        btq[:, 128 * hc:128 * hc + 128],
                                        usb[:, 512 * tc:512 * tc + 512],
                                        start=False, stop=True)
                    return f
                P.do("tensor", h13(w1b[par], b1tqb[par], u1sb, b1bank),
                     waits=[sg_at[eq_i], (S["vec"], UTC),
                            (S["scl"], max(0, S["scl"].n - 1)),
                            (S["oe"], max(0, S["oe"].n - 1))],
                     inc=S["mm"])
                P.do("scalar", (lambda hcg=hcg, b=b1bank: lambda E:
                                E.activation(gtmp1[:, :], pSIM[:, 512 * b:512 * b + 512],
                                             AF.Silu,
                                             bias=b1ce[:, hcg:hcg + 1], scale=1.0))(),
                     waits=[(S["mm"], S["mm"].n), (S["vec"], S["vec"].n)],
                     inc=S["scl"], drain=True)
                P.do("tensor", h13(w3b[par], b3tqb[par], u3sb, b3bank),
                     waits=[(S["scl2"], max(0, S["scl2"].n - 1)),
                            (S["tev"], S["tev"].n),
                            (S["oe"], S["oe"].n)], inc=S["sim"])
                P.do("scalar", (lambda hcg=hcg, b=b3bank: lambda E:
                                E.activation(gtmp2[:, :], pSIM[:, 512 * b:512 * b + 512],
                                             AF.Identity,
                                             bias=b3ce[:, hcg:hcg + 1], scale=1.0))(),
                     waits=[(S["sim"], S["sim"].n), (S["vec"], S["vec"].n)],
                     inc=S["scl2"], drain=True)
                P.do("vector", lambda E: E.tensor_mul(gtmp1[:, :], gtmp1[:, :],
                                                      gtmp2[:, :]),
                     waits=[(S["scl"], S["scl"].n), (S["scl2"], S["scl2"].n),
                            (S["mm2"], S["mm2"].n), (S["av"], S["av"].n)],
                     drain=True)
                P.do("vector", (lambda hc=hc, tc=tc: lambda E:
                                E.tensor_mul(gTq[:, 512 * hc:512 * hc + 512],
                                             gtmp1[:, :],
                                             gateb[:, 512 * tc:512 * tc + 512]))(),
                     inc=S["vec"], drain=True)
            def u2mm(par):
                def f(E):
                    ins = None
                    for hc in range(4):
                        ins = E.matmul(pMISC[32:48, 0:512],
                                       a2tqb[par][:, 16 * hc:16 * hc + 16],
                                       gTq[:, 512 * hc:512 * hc + 512],
                                       start=(hc == 0), stop=(hc == 3))
                    return ins
                return f
            GTC = S["vec"].n

            def emit_u2(tc=tc, par=par, q=q):
                P.do("tensor", u2mm(par), waits=[(S["vec"], GTC)], inc=S["mm2"])
                P.do("vector", (lambda tc=tc: lambda E:
                                E.scalar_tensor_tensor(
                                    u2sb[:, 512 * tc:512 * tc + 512],
                                    pMISC[32:48, 0:512], 1.0,
                                    u2sb[:, 512 * tc:512 * tc + 512],
                                    ALU.mult, ALU.add))(),
                     waits=[(S["mm2"], S["mm2"].n)], inc=S["vec3"], drain=True)
                if q == 3:
                    P.do("vector", (lambda tc=tc: lambda E:
                                    E.tensor_copy(u2bf[:, 512 * tc:512 * tc + 512],
                                                  u2sb[:, 512 * tc:512 * tc + 512]))(),
                         inc=S["vec3"], drain=True)

            if q == 3:
                emit_u2()
            for dc in range(8):
                ybank = 0 if dc % 2 == 0 else 1
                def ymm(dc=dc, tc=tc, par=par, q=q, eq_i=eq_i, yb=ybank):
                    def f(E):
                        ins = None
                        for hc in range(4):
                            last = (hc == 3) and q != 3 and eq_i != 3
                            ins = E.matmul(pch(yb),
                                           w2b[par][:, 1024 * hc + 128 * dc:
                                                    1024 * hc + 128 * dc + 128],
                                           gTq[:, 512 * hc:512 * hc + 512],
                                           start=(hc == 0), stop=last)
                        if q == 3:
                            ins = E.matmul(pch(yb),
                                           b2te[:, 128 * dc:128 * dc + 128],
                                           u2bf[:, 512 * tc:512 * tc + 512],
                                           start=False, stop=(eq_i != 3))
                        if eq_i == 3:
                            ins = E.matmul(pch(yb),
                                           b2allb[:, 128 * dc:128 * dc + 128],
                                           gatesF[:, 512 * tc:512 * tc + 512],
                                           start=False, stop=True)
                        return ins
                    return f
                yw = [(S["vec"], GTC), (S["oe"], max(0, S["oe"].n - 1)),
                      (S["scl"], S["scl"].n), (S["scl2"], S["scl2"].n)]
                if q == 3:
                    yw.append((S["vec3"], S["vec3"].n))
                P.do("tensor", ymm(), waits=yw, inc=S["av"])
                P.do("vector", (lambda dc=dc, tc=tc, yb=ybank: lambda E:
                                E.scalar_tensor_tensor(
                                    x1[:, 2048 * dc + 512 * tc:
                                       2048 * dc + 512 * tc + 512],
                                    pch(yb), 1.0,
                                    x1[:, 2048 * dc + 512 * tc:
                                       2048 * dc + 512 * tc + 512],
                                    ALU.mult, ALU.add))(),
                     waits=[(S["av"], S["av"].n)], inc=S["oe"], drain=True)
            if q != 3:
                emit_u2()
        P.do("tensor", lambda E: E.nop(),
             waits=[(S["scl"], S["scl"].n), (S["scl2"], S["scl2"].n),
                    (S["mm2"], S["mm2"].n), (S["av"], S["av"].n)],
             inc=S["mm4"])
    P.flush("g")

    for dc in range(8):
        P.dma("sync", out_d[128 * dc:128 * dc + 128, :],
              x1[:, 2048 * dc:2048 * dc + 2048],
              waits=[(S["oe"], S["oe"].n)], inc=S["fin"])
    P.do("sync", lambda E: E.nop(), waits=[(S["fin"], S["fin"].n)])
    P.flush("out")

    nc.finalize()
    return nc, sorted(TAPS.keys())


_BUILD_CACHE = {}


def kernel(**inputs):
    if "nc" not in _BUILD_CACHE:
        _BUILD_CACHE["nc"] = build()
    nc, _ = _BUILD_CACHE["nc"]
    CON = _host_constants()
    L = _host_layout(inputs)
    xTs = L.pop("_xT")
    static = {k: np.ascontiguousarray(v) for k, v in L.items()}
    for k, v in CON.items():
        static[k] = np.ascontiguousarray(v)
    in_maps = []
    for b in range(8):
        m = dict(static)
        m["xT"] = xTs[b]
        in_maps.append(m)
    res = run_bass_kernel_spmd(nc, in_maps, core_ids=list(range(8)))
    out = np.stack([np.ascontiguousarray(np.asarray(r["out"]).T)
                    for r in res.results])
    return out
